# version 01 (final)
# baseline (speedup 1.0000x reference)
import math
import numpy as np
import ml_dtypes
import concourse.bass as bass
import concourse.mybir as mybir
from concourse.bass_utils import run_bass_kernel_spmd
from concourse.ap import AP

F32 = mybir.dt.float32
BF16 = mybir.dt.bfloat16
U8 = mybir.dt.uint8
U32 = mybir.dt.uint32
I32 = mybir.dt.int32
AF = mybir.ActivationFunctionType
ALU = mybir.AluOpType
AX = mybir.AxisListType
BF = ml_dtypes.bfloat16

NCORES = 8
S = 4096
E = 1024


class Sem:
    __slots__ = ("h", "n")

    def __init__(self, h):
        self.h = h
        self.n = 0


class MSem:

    def __init__(self, p, name):
        self.p = p
        self.name = name
        self.subs = {}

    def sub(self, eng):
        ek = str(eng.engine)
        if ek not in self.subs:
            self.subs[ek] = self.p.sem(f"{self.name}_{ek.split('.')[-1]}")
        return self.subs[ek]

    def wait(self, eng):
        for s in self.subs.values():
            if s.n > 0:
                eng.wait_ge(s.h, s.n)


_ANCHOR_DENY = ("Semaphore", "Nop", "NoOp", "DMA", "Dma", "Event", "Branch",
                "Trigger", "Collective", "Switch", "Halt", "Drain")


class Prog:

    def __init__(self, name):
        self.name = name
        self.nc = bass.Bass("TRN2", target_bir_lowering=False, debug=False)
        self.in_specs = {}
        self.out_specs = {}
        self._semi = 0
        self._bar = None
        self._last = {}
        from concourse.bass_core_rust import BassInstruction as _BI
        self._BI = _BI

        def _cb(ins):
            n = type(ins).__name__
            if any(k in n for k in _ANCHOR_DENY):
                return
            self._last[str(ins.engine)] = ins
        self.nc.push_inst_callback(_cb)

    def inp(self, name, shape, dt=F32):
        t = self.nc.dram_tensor(name, list(shape), dt, kind="ExternalInput")
        self.in_specs[name] = (tuple(shape), dt)
        return t.ap()

    def outp(self, name, shape, dt=F32):
        t = self.nc.dram_tensor(name, list(shape), dt, kind="ExternalOutput")
        self.out_specs[name] = (tuple(shape), dt)
        return t.ap()

    def dram(self, name, shape, dt=F32):
        return self.nc.dram_tensor(name, list(shape), dt, kind="Internal").ap()

    def sb(self, name, shape, dt=F32):
        return self.nc.alloc_sbuf_tensor(name, list(shape), dt).ap()

    def ps(self, name, shape, dt=F32):
        return self.nc.alloc_psum_tensor(name, list(shape), dt).ap()

    def sem(self, name=None):
        self._semi += 1
        cm = self.nc.semaphore(name or f"s{self._semi}")
        return Sem(cm.__enter__())

    def sig(self, eng, sem, instr=None):
        if isinstance(sem, MSem):
            sem = sem.sub(eng)
        if not hasattr(self, "_anch"):
            self._anch = {}
            self._ancsem = {}
        if instr is None:
            ek = str(eng.engine)
            if ek not in self._ancsem:
                self._ancsem[ek] = self.sem(f"anc_{ek.split('.')[-1]}")
            anc = self._ancsem[ek]
            ins = self._last.get(ek)
            if ins is not None:
                if id(ins) not in self._anch:
                    self._BI(ins).then_inc(anc.h, 1)
                    anc.n += 1
                    self._anch[id(ins)] = anc.n
                eng.wait_ge(anc.h, self._anch[id(ins)])
            instr = eng.nop(nofuse=True)
        instr.then_inc(sem.h, 1)
        sem.n += 1
        return sem.n

    def chain(self, eng):
        if not hasattr(self, "_anch"):
            self._anch = {}
            self._ancsem = {}
        ek = str(eng.engine)
        if ek not in self._ancsem:
            self._ancsem[ek] = self.sem(f"anc_{ek.split('.')[-1]}")
        anc = self._ancsem[ek]
        ins = self._last.get(ek)
        if ins is None:
            return
        if id(ins) not in self._anch:
            self._BI(ins).then_inc(anc.h, 1)
            anc.n += 1
            self._anch[id(ins)] = anc.n
        eng.wait_ge(anc.h, self._anch[id(ins)])

    def barrier(self, engines=None):
        engines = engines or self.eng_all()
        if not hasattr(self, "_bars"):
            self._bars = {}
        key = tuple(sorted(str(e.engine) for e in engines))
        if key not in self._bars:
            self._bars[key] = self.sem(f"bar{len(self._bars)}")
        bar = self._bars[key]
        for e in engines:
            self.sig(e, bar)
        for e in engines:
            e.wait_ge(bar.h, bar.n)

    def eng_all(self):
        nc = self.nc
        return [nc.sync, nc.tensor, nc.vector, nc.scalar, nc.gpsimd]

    def constcol(self, val, dt=F32):
        if not hasattr(self, "_ccols"):
            self._ccols = {}
        key = (float(val), dt)
        if key not in self._ccols:
            t = self.sb(f"cc{len(self._ccols)}", [128, 1], dt)
            self.nc.vector.memset(t[:, :], float(val))
            self._ccols[key] = t
        return self._ccols[key]



def dram_slice(w, r0, nr, c0, nc_):
    W = w.tensor.shape[-1]
    return AP(tensor=w.tensor, offset=r0 * W + c0, ap=[[W, nr], [1, nc_]])


def emit_ln_stats(p, xv, stats, scol, feat, eps):
    nc = p.nc
    c0, c1, c2, c3 = (stats[:, scol + i:scol + i + 1] for i in range(4))
    nc.scalar.activation(out=p._lnsq[:, 0:feat], in_=xv, func=AF.Square,
                         accum_out=c1)
    nc.vector.reduce_sum(c0, xv, axis=AX.X)
    p.barrier([nc.vector, nc.scalar])
    nc.vector.tensor_scalar(out=c0, in0=c0, scalar1=1.0 / feat, scalar2=None,
                            op0=ALU.mult)
    nc.vector.tensor_scalar(out=c1, in0=c1, scalar1=1.0 / feat, scalar2=None,
                            op0=ALU.mult)
    p.chain(nc.vector)
    nc.vector.tensor_mul(out=c2, in0=c0, in1=c0)
    p.chain(nc.vector)
    nc.vector.tensor_sub(out=c1, in0=c1, in1=c2)
    p.barrier([nc.vector, nc.scalar])
    nc.scalar.activation(out=c3, in_=c1, func=AF.Ln,
                         bias=p.constcol(eps)[0:xv.shape[0], 0:1])
    p.barrier([nc.vector, nc.scalar])
    nc.scalar.activation(out=c3, in_=c3, func=AF.Exp,
                         scale=p.constcol(-0.5)[0:xv.shape[0], 0:1])
    p.barrier([nc.vector, nc.scalar])
    nc.vector.tensor_mul(out=c2, in0=c0, in1=c3)
    p.chain(nc.vector)
    nc.vector.tensor_scalar(out=c2, in0=c2, scalar1=-1.0, scalar2=None,
                            op0=ALU.mult)
    p.barrier([nc.vector, nc.scalar])
    return c2, c3



def build_attn01(layer, phases=99):
    p = Prog(f"l{layer}attn")
    nc = p.nc
    HH = 4
    Dh = 64
    scale = Dh ** -0.5
    KB = 4

    x = p.inp("x", [S, E])
    Wq = p.inp("Wq", [E, 512]); Wk = p.inp("Wk", [E, 512]); Wv = p.inp("Wv", [E, 512])
    Wo = p.inp("Wo", [512, E])
    bq = p.inp("bq", [128, 4]); bk = p.inp("bk", [128, 4]); bv = p.inp("bv", [128, 4])
    bo = p.inp("bo", [1, E])
    idn_in = p.inp("idn", [128, 128], BF16)
    idnf_in = p.inp("idnf", [128, 128], F32)
    tri_in = p.inp("tri", [128, 128], BF16)
    if layer == 0:
        dp_w1 = p.inp("dp_w1", [E, 512]); dp_b1 = p.inp("dp_b1", [1, 512])
        ln_g = p.inp("ln_g", [1, 512]); ln_b = p.inp("ln_b", [1, 512])
        dp_w2 = p.inp("dp_w2", [128, 4]); dp_b2 = p.inp("dp_b2", [1, 1])
        g_w1 = p.inp("g_w1", [1, 16]); g_b1 = p.inp("g_b1", [16, 1])
        g_w2 = p.inp("g_w2", [16, 1]); g_b2 = p.inp("g_b2", [1, 1])
    else:
        alc_in = p.inp("alc", [128, 8 * 32])
        asl_in = p.inp("asl", [1, 8])
    part = p.outp("part", [S, E])

    Qaug = p.sb("Qaug", [128, HH * S], BF16)
    Kaug = p.sb("Kaug", [128, HH * S], BF16)
    Vb = p.sb("Vb", [128, 32 * HH * 65], BF16)
    aoT = p.sb("aoT", [128, 4 * S], BF16)
    xnT = p.sb("xnT", [128, 8 * 512], BF16)
    PT = p.sb("PT", [128, 2 * KB * 512], BF16)
    x_tok = p.sb("x_tok", [128, E])
    xn_tok = p.sb("xn_tok", [128, E], BF16)
    p._lnsq = xn_tok
    stats = p.sb("stats", [128, 16])
    idn = p.sb("idn_s", [128, 128], BF16)
    idnf = p.sb("idnf_s", [128, 128], F32)
    tri = p.sb("tri_s", [128, 128], BF16)
    fneg = p.sb("fneg_s", [128, 128], BF16)
    ones11f = p.sb("ones11f", [1, 1], F32)
    ones64b = p.sb("ones64b", [64, 1], BF16)
    ones1f = p.sb("ones1f", [1, 128], F32)
    ones64f = p.sb("ones64f", [64, 1], F32)
    wq_s = p.sb("wq_s", [128, 8 * 512], BF16)
    wk_s = p.sb("wk_s", [128, 8 * 512], BF16)
    wv_s = p.sb("wv_s", [128, 8 * 512], BF16)
    wo_s = p.sb("wo_s", [128, 4 * E], BF16)
    bqs_s = p.sb("bqs_s", [128, 4])
    bk_s = p.sb("bk_s", [128, 4])
    bv_s = p.sb("bv_s", [128, 4])
    bo_s = p.sb("bo_s", [1, E])
    vt_scr = p.sb("vt_scr", [128, 512], BF16)
    qsq = p.sb("qsq", [64, 512])
    tr1 = p.sb("tr1", [1, 1])
    kn2 = p.sb("kn2", [1, 16])
    dn_tmp = p.sb("dn_tmp", [1, 512])
    den_pc = p.sb("den_pc", [128, 8 * 32])
    rr_tmp = p.sb("rr_tmp", [1, 512])
    crow = rr_tmp
    outbuf = p.sb("outbuf", [128, E])
    if layer == 0:
        dpw1_s = p.sb("dpw1_s", [128, 8 * 512], BF16)
        dpw2_s = p.sb("dpw2_s", [128, 4], BF16)
        gw1_s = p.sb("gw1_s", [1, 16], BF16)
        gw2_s = p.sb("gw2_s", [16, 1], BF16)
        gb1_s = p.sb("gb1_s", [16, 1])
        bb1 = p.sb("bb1", [128, 512])
        lgB = p.sb("lgB", [128, 512])
        lbB = p.sb("lbB", [128, 512])
        db2_s = p.sb("db2_s", [1, 1])
        gb2_s = p.sb("gb2_s", [1, 1])
        h_tok = p.sb("h_tok", [128, 512])
        hbf = p.sb("hbf", [128, 512], BF16)
        ghT = p.sb("ghT", [128, 512], BF16)
        grow = p.sb("grow", [1, 512])
        gcol = p.sb("gcol", [128, 32])
        gsc = p.sb("gsc", [16, 512], BF16)
        drow = p.sb("drow", [1, 512], BF16)

    else:
        alc = p.sb("alc_s", [128, 8 * 32])
        asl = p.sb("asl_s", [1, 8])

    psL = p.ps("psL", [128, KB * 512])
    ps1 = p.ps("ps1", [128, 512])
    ps2 = p.ps("ps2", [128, 512])
    psAV = p.ps("psAV", [128, 512])
    ptb = p.ps("ptb", [128, 128], BF16)

    ld = p.sem("ld")
    cs = p.sem("cs")
    mm = p.sem("mm")
    ev = MSem(p, "ev")
    p.constcol(1e-5); p.constcol(-0.5)

    ld.n += 16; nc.sync.dma_start(out=idn[:], in_=idn_in[:, :]).then_inc(ld.h, 16)
    ld.n += 16; nc.sync.dma_start(out=tri[:], in_=tri_in[:, :]).then_inc(ld.h, 16)
    ld.n += 16; nc.sync.dma_start(out=idnf[:], in_=idnf_in[:, :]).then_inc(ld.h, 16)
    nc.vector.memset(ones64b[:], 1.0)
    nc.vector.memset(ones11f[:], 1.0)
    nc.vector.memset(fneg[:], -1e30)
    nc.vector.memset(ones1f[:], 1.0)
    nc.vector.memset(ones64f[:], 1.0)
    nc.vector.memset(kn2[:], 0.0)
    for kt in range(8):
        for (w, ws) in ((Wq, wq_s), (Wk, wk_s), (Wv, wv_s)):
            cs.n += 16
            nc.gpsimd.dma_start(out=ws[:, kt * 512:(kt + 1) * 512],
                                in_=dram_slice(w, kt * 128, 128, 0, 512)
                                ).then_inc(cs.h, 16)
    for kt in range(4):
        cs.n += 16
        nc.gpsimd.dma_start(out=wo_s[:, kt * E:(kt + 1) * E],
                            in_=dram_slice(Wo, kt * 128, 128, 0, E)
                            ).then_inc(cs.h, 16)
    for (b, bs) in ((bq, bqs_s), (bk, bk_s), (bv, bv_s)):
        ld.n += 16; nc.sync.dma_start(out=bs[:, :], in_=b[:, :]).then_inc(ld.h, 16)
    ld.n += 16; nc.sync.dma_start(out=bo_s[:, :], in_=bo[:, :]).then_inc(ld.h, 16)
    if layer == 0:
        for kt in range(8):
            cs.n += 16
            nc.gpsimd.dma_start(out=dpw1_s[:, kt * 512:(kt + 1) * 512],
                                in_=dram_slice(dp_w1, kt * 128, 128, 0, 512)
                                ).then_inc(cs.h, 16)
        cs.n += 16; nc.gpsimd.dma_start(out=dpw2_s[:, :], in_=dp_w2[:, :]).then_inc(cs.h, 16)
        cs.n += 16; nc.gpsimd.dma_start(out=gw1_s[:, :], in_=g_w1[:, :]).then_inc(cs.h, 16)
        cs.n += 16; nc.gpsimd.dma_start(out=gw2_s[:, :], in_=g_w2[:, :]).then_inc(cs.h, 16)
        ld.n += 16; nc.sync.dma_start(out=gb1_s[:, :], in_=g_b1[:, :]).then_inc(ld.h, 16)
        ld.n += 16; nc.sync.dma_start(out=db2_s[:, :], in_=dp_b2[:, :]).then_inc(ld.h, 16)
        ld.n += 16; nc.sync.dma_start(out=gb2_s[:, :], in_=g_b2[:, :]).then_inc(ld.h, 16)
    else:
        ld.n += 16; nc.sync.dma_start(out=alc[:, :], in_=alc_in[:, :]).then_inc(ld.h, 16)
        ld.n += 16; nc.sync.dma_start(out=asl[:, :], in_=asl_in[:, :]).then_inc(ld.h, 16)
    for e in p.eng_all():
        e.wait_ge(ld.h, ld.n)
        e.wait_ge(cs.h, cs.n)
    p.barrier()
    if phases <= 1:
        return _finish_debug(p, nc, part, outbuf, ld)
    if layer == 0:
        for (r, bt) in ((dp_b1, bb1), (ln_g, lgB), (ln_b, lbB)):
            ld.n += 16
            nc.sync.dma_start(out=dn_tmp[0:1, :], in_=r[:, :]).then_inc(ld.h, 16)
            nc.tensor.wait_ge(ld.h, ld.n)
            nc.tensor.matmul(ps1[:, :], ones1f[0:1, :], dn_tmp[0:1, :],
                             start=True, stop=True)
            p.sig(nc.tensor, mm)
            nc.vector.wait_ge(mm.h, mm.n)
            nc.vector.tensor_copy(out=bt[:, :], in_=ps1[:, :])
            p.sig(nc.vector, ev)
            ev.wait(nc.tensor)
            ev.wait(nc.sync)
    p.barrier()

    if phases <= 2:
        return _finish_debug(p, nc, part, outbuf, ld)
    for hp in range(2):
        for g in range(8):
            g0 = g * 512
            for tt in range(4):
                t0 = g0 + tt * 128
                xv = x_tok[:, :]
                p.barrier([nc.sync, nc.vector, nc.scalar])
                ld.n += 16
                nc.sync.dma_start(out=xv, in_=x[t0:t0 + 128, :]).then_inc(ld.h, 16)
                nc.vector.wait_ge(ld.h, ld.n)
                nc.scalar.wait_ge(ld.h, ld.n)
                bcol, scol = emit_ln_stats(p, xv, stats, 4 * (tt % 2), E, 1e-5)
                nc.vector.wait_ge(mm.h, mm.n)
                nc.vector.tensor_scalar(out=xn_tok[:, :], in0=xv, scalar1=scol,
                                        scalar2=bcol, op0=ALU.mult, op1=ALU.add)
                p.sig(nc.vector, ev)
                ev.wait(nc.tensor)
                for kt in range(8):
                    nc.tensor.transpose(ptb[:, :], xn_tok[:, kt * 128:(kt + 1) * 128],
                                        idn[:, :])
                    p.sig(nc.tensor, mm)
                    nc.vector.wait_ge(mm.h, mm.n)
                    nc.vector.tensor_copy(
                        out=xnT[:, kt * 512 + tt * 128: kt * 512 + (tt + 1) * 128],
                        in_=ptb[:, :])
                    p.sig(nc.vector, ev)
                    ev.wait(nc.tensor)
            p.barrier()
            for (ws, dst, bias_s, sc) in ((wq_s, Qaug, bqs_s, scale),
                                          (wk_s, Kaug, bk_s, 1.0)):
                for f in range(2):
                    c0 = hp * 256 + f * 128
                    for kt in range(8):
                        nc.tensor.matmul(ps1[:, :],
                                         ws[:, kt * 512 + c0: kt * 512 + c0 + 128],
                                         xnT[:, kt * 512:(kt + 1) * 512],
                                         start=(kt == 0), stop=(kt == 7))
                    p.sig(nc.tensor, mm)
                    nc.vector.wait_ge(mm.h, mm.n)
                    for hh2 in range(2):
                        h = f * 2 + hh2
                        bcol = bias_s[:, hp * 2 + f: hp * 2 + f + 1]
                        nc.vector.tensor_scalar(
                            out=dst[0:64, h * S + g0: h * S + g0 + 512],
                            in0=ps1[hh2 * 64:(hh2 + 1) * 64, :],
                            scalar1=bcol[hh2 * 64: hh2 * 64 + 64, 0:1],
                            scalar2=float(sc), op0=ALU.add, op1=ALU.mult)
                    p.sig(nc.vector, ev)
                    ev.wait(nc.tensor)
            for f in range(2):
                c0 = hp * 256 + f * 128
                for kt in range(8):
                    nc.tensor.matmul(ps2[:, :],
                                     wv_s[:, kt * 512 + c0: kt * 512 + c0 + 128],
                                     xnT[:, kt * 512:(kt + 1) * 512],
                                     start=(kt == 0), stop=(kt == 7))
                p.sig(nc.tensor, mm)
                nc.vector.wait_ge(mm.h, mm.n)
                nc.vector.tensor_scalar(out=vt_scr[:, :], in0=ps2[:, :],
                                        scalar1=bv_s[:, hp * 2 + f: hp * 2 + f + 1],
                                        scalar2=None, op0=ALU.add)
                p.sig(nc.vector, ev)
                ev.wait(nc.tensor)
                for tt in range(4):
                    ktg = g * 4 + tt
                    nc.tensor.transpose(ptb[:, :], vt_scr[:, tt * 128:(tt + 1) * 128],
                                        idn[:, :])
                    p.sig(nc.tensor, mm)
                    nc.vector.wait_ge(mm.h, mm.n)
                    for hh2 in range(2):
                        h = f * 2 + hh2
                        nc.vector.tensor_copy(
                            out=Vb[:, ktg * HH * 65 + h * 65: ktg * HH * 65 + h * 65 + 64],
                            in_=ptb[:, hh2 * 64:(hh2 + 1) * 64])
                    p.sig(nc.vector, ev)
                    ev.wait(nc.tensor)
            p.barrier()
            for h in range(HH):
                nc.scalar.wait_ge(mm.h, mm.n)
                nc.scalar.activation(out=qsq[0:64, :],
                                     in_=Kaug[0:64, h * S + g0: h * S + g0 + 512],
                                     func=AF.Square)
                p.sig(nc.scalar, ev)
                ev.wait(nc.tensor)
                nc.tensor.matmul(ps2[0:1, :], ones64f[0:64, :], qsq[0:64, :],
                                 start=True, stop=True)
                p.sig(nc.tensor, mm)
                nc.vector.wait_ge(mm.h, mm.n)
                nc.vector.reduce_max(tr1[0:1, 0:1], ps2[0:1, 0:512], axis=AX.X)
                p.chain(nc.vector)
                hcol = kn2[0:1, hp * HH + h: hp * HH + h + 1]
                if g == 0:
                    nc.vector.tensor_copy(out=hcol, in_=tr1[0:1, 0:1])
                else:
                    nc.vector.tensor_max(out=hcol, in0=hcol, in1=tr1[0:1, 0:1])
                p.sig(nc.vector, ev)
                ev.wait(nc.tensor)
            if layer == 0 and hp == 0:
                _drift_group(p, nc, g0, xnT, dpw1_s, bb1, lgB, lbB, dpw2_s, db2_s,
                             gw1_s, gb1_s, gw2_s, gb2_s, h_tok, hbf, ghT, stats,
                             grow, gsc, drow, ps1, ps2, ptb, idn, mm, ev,
                             (gcol, ones11f), (ld,))
            p.barrier()
            if phases <= 3 and g >= (phases_g := 0):
                pass
        if phases <= 3:
            return _finish_debug(p, nc, part, outbuf, ld)
        nc.vector.memset(Kaug[64:65, :], 1.0)
        if layer == 0:
            for kt in range(32):
                va = AP(tensor=Vb.tensor, offset=Vb.offset + kt * HH * 65,
                        ap=[[Vb.shape[1], 128], [65, HH], [1, 64]])
                nc.vector.tensor_scalar(out=va, in0=va, scalar1=gcol[:, kt:kt + 1],
                                        scalar2=None, op0=ALU.mult)
        for kt in range(32):
            va = AP(tensor=Vb.tensor, offset=Vb.offset + kt * HH * 65 + 64,
                    ap=[[Vb.shape[1], 128], [65, HH], [1, 1]])
            nc.vector.memset(va, 1.0)
        p.barrier()
        for h in range(HH):
            for g in range(8):
                cpos = slice(h * S + g * 512, h * S + (g + 1) * 512)
                nc.scalar.wait_ge(mm.h, mm.n)
                nc.scalar.activation(out=qsq[0:64, :], in_=Qaug[0:64, cpos],
                                     func=AF.Square)
                p.sig(nc.scalar, ev)
                ev.wait(nc.tensor)
                nc.tensor.matmul(ps2[0:1, :], ones64f[0:64, :], qsq[0:64, :],
                                 start=True, stop=True)
                p.sig(nc.tensor, mm)
                nc.scalar.wait_ge(mm.h, mm.n)
                nc.scalar.activation(out=crow[0:1, :], in_=ps2[0:1, 0:512],
                                     func=AF.Sqrt,
                                     scale=kn2[0:1, hp * HH + h: hp * HH + h + 1])
                p.sig(nc.scalar, ev)
                ev.wait(nc.vector)
                if layer == 1:
                    hg_ = hp * HH + h
                    ev.wait(nc.gpsimd)
                    nc.gpsimd.iota(dn_tmp[0:1, :], pattern=[[1, 512]],
                                   base=g * 512, channel_multiplier=0,
                                   allow_small_or_imprecise_dtypes=True)
                    p.sig(nc.gpsimd, ev)
                    ev.wait(nc.vector)
                    nc.vector.scalar_tensor_tensor(
                        out=crow[0:1, :], in0=dn_tmp[0:1, :],
                        scalar=asl[0:1, hg_:hg_ + 1],
                        in1=crow[0:1, :], op0=ALU.mult, op1=ALU.add)
                    p.chain(nc.vector)
                nc.vector.tensor_scalar(out=Qaug[64:65, cpos], in0=crow[0:1, :],
                                        scalar1=-1.0, scalar2=None, op0=ALU.mult)
                p.sig(nc.vector, ev)
                ev.wait(nc.tensor)
                ev.wait(nc.scalar)
        p.barrier()
        if phases <= 4:
            return _finish_debug(p, nc, part, outbuf, ld)
        for h in range(HH):
            hg = hp * HH + h
            hS = h * S
            for qc in range(8):
                nkt = min(4 * qc + 4, 32)
                q0 = qc * 512
                nbat = (nkt + KB - 1) // KB
                mm_done = [None] * nbat
                exp_done = [None] * nbat
                for b in range(nbat):
                    kts = range(b * KB, min((b + 1) * KB, nkt))
                    for kt in kts:
                        j = kt % KB
                        diag_off = kt * 128 - q0
                        mask_ops = []
                        if 0 <= diag_off < 512:
                            for j0 in range(0, diag_off, 128):
                                mask_ops.append((j0, fneg))
                            mask_ops.append((diag_off, tri))
                        nc.tensor.matmul(psL[:, j * 512:(j + 1) * 512],
                                         Kaug[0:65, hS + kt * 128: hS + (kt + 1) * 128],
                                         Qaug[0:65, hS + q0: hS + q0 + 512],
                                         start=True, stop=not mask_ops)
                        for mi, (j0, mt) in enumerate(mask_ops):
                            nc.tensor.matmul(
                                psL[:, j * 512 + j0: j * 512 + j0 + 128],
                                idn[:, :], mt[:, :], start=False,
                                stop=(mi == len(mask_ops) - 1))
                    mm_done[b] = p.sig(nc.tensor, mm)
                    nc.scalar.wait_ge(mm.h, mm_done[b])
                    pb0 = (b % 2) * KB * 512
                    if layer == 1:
                        for kt in kts:
                            j = kt % KB
                            nc.scalar.activation(
                                out=PT[:, pb0 + j * 512: pb0 + (j + 1) * 512],
                                in_=psL[:, j * 512:(j + 1) * 512], func=AF.Exp,
                                bias=alc[:, hg * 32 + kt: hg * 32 + kt + 1])
                    else:
                        nlast = len(list(kts))
                        nc.scalar.activation(
                            out=PT[:, pb0: pb0 + nlast * 512],
                            in_=psL[:, 0: nlast * 512], func=AF.Exp)
                    exp_done[b] = p.sig(nc.scalar, ev)
                    nc.tensor.wait_ge(ev.sub(nc.scalar).h, exp_done[b])
                    for kt in kts:
                        j = kt % KB
                        nc.tensor.matmul(psAV[0:65, :],
                                         Vb[:, kt * HH * 65 + h * 65: kt * HH * 65 + (h + 1) * 65],
                                         PT[:, pb0 + j * 512: pb0 + (j + 1) * 512],
                                         start=(kt == 0), stop=(kt == nkt - 1))
                p.sig(nc.tensor, mm)
                nc.scalar.wait_ge(mm.h, mm.n)
                nc.vector.wait_ge(mm.h, mm.n)
                pt = hg // 2
                r0 = (hg % 2) * 64
                nc.scalar.activation(out=aoT[r0:r0 + 64, pt * S + q0: pt * S + q0 + 512],
                                     in_=psAV[0:64, :], func=AF.Copy)
                dslot = dn_tmp[0:1, :]
                nc.vector.tensor_copy(out=dslot, in_=psAV[64:65, :])
                p.sig(nc.scalar, ev)
                p.sig(nc.vector, ev)
                ev.wait(nc.tensor)
                for tt in range(4):
                    nc.tensor.matmul(ps2[:, tt:tt + 1],
                                     dslot[0:1, tt * 128:(tt + 1) * 128],
                                     ones11f[0:1, 0:1],
                                     start=(tt == 0), stop=(tt == 3))
                p.sig(nc.tensor, mm)
                nc.vector.wait_ge(mm.h, mm.n)
                nc.vector.tensor_copy(
                    out=den_pc[:, hg * 32 + qc * 4: hg * 32 + qc * 4 + 4],
                    in_=ps2[:, 0:4])
                p.sig(nc.vector, ev)
                ev.wait(nc.tensor)
        p.barrier()
        if phases <= 5:
            return _finish_debug(p, nc, part, outbuf, ld)

    nc.vector.reciprocal(out=den_pc[:, :], in_=den_pc[:, :])
    p.barrier()
    for hg in range(8):
        pt = hg // 2
        r0 = (hg % 2) * 64
        for qc in range(8):
            q0 = qc * 512
            for tt in range(4):
                c = hg * 32 + qc * 4 + tt
                nc.tensor.matmul(ps2[0:1, tt * 128:(tt + 1) * 128],
                                 den_pc[:, c:c + 1], idnf[:, :],
                                 start=True, stop=True)
            p.sig(nc.tensor, mm)
            nc.vector.wait_ge(mm.h, mm.n)
            nc.vector.tensor_copy(out=rr_tmp[0:1, :], in_=ps2[0:1, 0:512])
            p.sig(nc.vector, ev)
            ev.wait(nc.tensor)
            nc.tensor.matmul(ps1[:, :], ones1f[0:1, :], rr_tmp[0:1, :],
                             start=True, stop=True)
            p.sig(nc.tensor, mm)
            nc.vector.wait_ge(mm.h, mm.n)
            a = aoT[r0:r0 + 64, pt * S + q0: pt * S + q0 + 512]
            nc.vector.tensor_mul(out=a, in0=a, in1=ps1[0:64, :])
            p.sig(nc.vector, ev)
            ev.wait(nc.tensor)
    p.barrier()
    for tt in range(32):
        t0 = tt * 128
        ev.wait(nc.tensor)
        for oc in range(2):
            pso = ps1 if oc == 0 else ps2
            for f in range(4):
                nc.tensor.matmul(pso[:, :], aoT[:, f * S + t0: f * S + t0 + 128],
                                 wo_s[:, f * E + oc * 512: f * E + (oc + 1) * 512],
                                 start=(f == 0), stop=False)
            nc.tensor.matmul(pso[:, :], ones1f[0:1, :],
                             bo_s[0:1, oc * 512:(oc + 1) * 512], start=False,
                             stop=True)
        p.sig(nc.tensor, mm)
        nc.vector.wait_ge(mm.h, mm.n)
        nc.vector.wait_ge(ld.h, ld.n)
        ob = outbuf[:, :]
        nc.vector.tensor_copy(out=ob[:, 0:512], in_=ps1[:, :])
        nc.vector.tensor_copy(out=ob[:, 512:1024], in_=ps2[:, :])
        p.sig(nc.vector, ev)
        ev.wait(nc.sync)
        ld.n += 16
        nc.sync.dma_start(out=part[t0:t0 + 128, :], in_=ob).then_inc(ld.h, 16)
    nc.sync.wait_ge(ld.h, ld.n)
    p.barrier()
    return p


def _drift_group(p, nc, g0, xnT, dpw1_s, bb1, lgB, lbB, dpw2_s, db2_s, gw1_s,
                 gb1_s, gw2_s, gb2_s, h_tok, hbf, ghT, stats, grow, gsc, drow,
                 ps1, ps2, ptb, idn, mm, ev, env_gcol, env_ld):
    for tt in range(4):
        for kt in range(8):
            nc.tensor.matmul(ps1[:, :],
                             xnT[:, kt * 512 + tt * 128: kt * 512 + (tt + 1) * 128],
                             dpw1_s[:, kt * 512:(kt + 1) * 512],
                             start=(kt == 0), stop=(kt == 7))
        p.sig(nc.tensor, mm)
        nc.vector.wait_ge(mm.h, mm.n)
        nc.vector.tensor_add(out=h_tok[:, :], in0=ps1[:, :], in1=bb1[:, :])
        p.sig(nc.vector, ev)
        ev.wait(nc.tensor)
        ev.wait(nc.scalar)
        hv = h_tok[:, :]
        bcol, scol = emit_ln_stats(p, hv, stats, 0, 512, 1e-5)
        nc.vector.tensor_scalar(out=hv, in0=hv, scalar1=scol, scalar2=bcol,
                                op0=ALU.mult, op1=ALU.add)
        p.chain(nc.vector)
        nc.vector.tensor_mul(out=hv, in0=hv, in1=lgB[:, :])
        p.chain(nc.vector)
        nc.vector.tensor_add(out=hv, in0=hv, in1=lbB[:, :])
        p.sig(nc.vector, ev)
        ev.wait(nc.scalar)
        nc.scalar.activation(out=hbf[:, :], in_=hv, func=AF.Gelu)
        p.sig(nc.scalar, ev)
        ev.wait(nc.tensor)
        for ft in range(4):
            nc.tensor.transpose(ptb[:, :], hbf[:, ft * 128:(ft + 1) * 128], idn[:, :])
            p.sig(nc.tensor, mm)
            nc.vector.wait_ge(mm.h, mm.n)
            nc.vector.tensor_copy(out=ghT[:, ft * 128:(ft + 1) * 128], in_=ptb[:, :])
            p.sig(nc.vector, ev)
            ev.wait(nc.tensor)
        for ft in range(4):
            nc.tensor.matmul(ps2[0:1, tt * 128:(tt + 1) * 128],
                             dpw2_s[:, ft:ft + 1], ghT[:, ft * 128:(ft + 1) * 128],
                             start=(ft == 0), stop=(ft == 3))
        p.sig(nc.tensor, mm)
    p.barrier([nc.vector, nc.scalar, nc.tensor])
    nc.vector.tensor_scalar(out=drow[0:1, :], in0=ps2[0:1, 0:512],
                            scalar1=db2_s[0:1, 0:1], scalar2=None, op0=ALU.add)
    p.sig(nc.vector, ev)
    ev.wait(nc.tensor)
    ev.wait(nc.scalar)
    nc.tensor.matmul(ps2[0:16, :], gw1_s[0:1, :], drow[0:1, :], start=True, stop=True)
    p.sig(nc.tensor, mm)
    nc.scalar.wait_ge(mm.h, mm.n)
    nc.scalar.activation(out=gsc[0:16, :], in_=ps2[0:16, 0:512], func=AF.Gelu,
                         bias=gb1_s[0:16, 0:1])
    p.sig(nc.scalar, ev)
    ev.wait(nc.tensor)
    nc.tensor.matmul(ps2[0:1, :], gw2_s[0:16, 0:1], gsc[0:16, :], start=True,
                     stop=True)
    p.sig(nc.tensor, mm)
    nc.scalar.wait_ge(mm.h, mm.n)
    nc.scalar.activation(out=grow[0:1, :], in_=ps2[0:1, 0:512],
                         func=AF.Tanh, bias=gb2_s[0:1, 0:1])
    p.sig(nc.scalar, ev)
    ev.wait(nc.tensor)
    gcol = env_gcol[0]
    ones11f = env_gcol[1]
    for tt in range(4):
        nc.tensor.matmul(ps1[:, tt:tt + 1], grow[0:1, tt * 128:(tt + 1) * 128],
                         ones11f[0:1, 0:1], start=(tt == 0), stop=(tt == 3))
    p.sig(nc.tensor, mm)
    nc.vector.wait_ge(mm.h, mm.n)
    nc.vector.tensor_copy(out=gcol[:, (g0 // 512) * 4:(g0 // 512) * 4 + 4],
                          in_=ps1[:, 0:4])
    p.sig(nc.vector, ev)
    ev.wait(nc.tensor)
    ev.wait(nc.scalar)


def _finish_debug(p, nc, part, outbuf, ld):
    p.barrier()
    nc.vector.memset(outbuf[:, :], 0.0)
    p.barrier()
    for tt in range(32):
        ld.n += 16
        nc.sync.dma_start(out=part[tt * 128:(tt + 1) * 128, :],
                          in_=outbuf[:, :]).then_inc(ld.h, 16)
    nc.sync.wait_ge(ld.h, ld.n)
    p.barrier()
    return p



def host_tri_mask():
    m = np.zeros((128, 128), np.float32)
    k = np.arange(128)[:, None]
    q = np.arange(128)[None, :]
    m[k > q] = -1e30
    return m.astype(BF)


def host_identity():
    return np.eye(128, dtype=BF)


def host_alibi_cols(h0):
    out = np.zeros((128, 8 * 32), np.float32)
    for h in range(8):
        gh = h0 + h + 1
        slope = 2.0 ** (-8.0 * gh / 16.0) / 8.0
        for kt in range(32):
            out[:, h * 32 + kt] = slope * (kt * 128 + np.arange(128))
    return out



def attn01_in_maps(layer, xfull, w):
    idn = host_identity()
    tri = host_tri_mask()
    maps = []
    for c in range(NCORES):
        b, hh = c // 2, c % 2
        h0 = hh * 8
        cs = h0 * 64
        m = {"x": np.ascontiguousarray(xfull[b]),
             "idn": idn, "idnf": np.eye(128, dtype=np.float32), "tri": tri}
        if layer == 0:
            m["Wq"] = np.ascontiguousarray(w["Wq"][:, cs:cs + 512])
            m["Wk"] = np.ascontiguousarray(w["Wk"][:, cs:cs + 512])
            m["Wv"] = np.ascontiguousarray(w["Wv"][:, cs:cs + 512])
            m["Wo"] = np.ascontiguousarray(w["Wo"][cs:cs + 512, :])
            m["bq"] = np.ascontiguousarray(w["bq"][cs:cs + 512].reshape(4, 128).T)
            m["bk"] = np.ascontiguousarray(w["bk"][cs:cs + 512].reshape(4, 128).T)
            m["bv"] = np.ascontiguousarray(w["bv"][cs:cs + 512].reshape(4, 128).T)
            m["bo"] = (w["bo"].reshape(1, E) if hh == 0 else
                       np.zeros((1, E), np.float32)).copy()
            m["dp_w1"] = w["dp_w1"]
            m["dp_b1"] = w["dp_b1"].reshape(1, 512)
            m["ln_g"] = w["ln_g"].reshape(1, 512)
            m["ln_b"] = w["ln_b"].reshape(1, 512)
            m["dp_w2"] = np.ascontiguousarray(w["dp_w2"].reshape(4, 128).T)
            m["dp_b2"] = w["dp_b2"].reshape(1, 1)
            m["g_w1"] = w["g_w1"]
            m["g_b1"] = w["g_b1"].reshape(16, 1)
            m["g_w2"] = w["g_w2"]
            m["g_b2"] = w["g_b2"].reshape(1, 1)
        else:
            m["Wq"] = np.ascontiguousarray(w["W_query"][:, cs:cs + 512])
            m["Wk"] = np.ascontiguousarray(w["W_key"][:, cs:cs + 512])
            m["Wv"] = np.ascontiguousarray(w["W_value"][:, cs:cs + 512])
            m["Wo"] = np.ascontiguousarray(w["W_proj"][cs:cs + 512, :])
            z = np.zeros((128, 4), np.float32)
            m["bq"] = z; m["bk"] = z; m["bv"] = z
            m["bo"] = (w["b_proj"].reshape(1, E) if hh == 0 else
                       np.zeros((1, E), np.float32)).copy()
            m["alc"] = host_alibi_cols(h0)
            m["asl"] = np.array([[2.0 ** (-8.0 * (h0 + j + 1) / 16.0) / 8.0
                                  for j in range(8)]], np.float32)
        maps.append(m)
    return maps


def run_launch(p, in_maps, **kw):
    res = run_bass_kernel_spmd(p.nc, in_maps, core_ids=list(range(NCORES)), **kw)
    return res.results


def kernel(**inputs):
    ins = {k: np.asarray(v) for k, v in inputs.items()}
    x0 = np.ascontiguousarray(ins["L0_attn_x"], np.float32)
    idn = host_identity()
    idnf = np.eye(128, dtype=np.float32)

    def tok(c):
        return c // 2, slice((c % 2) * TS, (c % 2 + 1) * TS)

    def asm_tok(res, key):
        out = np.zeros((4, S, E), np.float32)
        for c in range(NCORES):
            b, sl = tok(c)
            out[b, sl] = np.asarray(res[c][key])
        return out

    w0 = {k[len("L0_attn_"):]: ins[k] for k in ins
          if k.startswith("L0_attn_") and k != "L0_attn_x"}
    r = run_launch(build_attn01(0), attn01_in_maps(0, x0, w0))
    parts = [np.asarray(r[c]["part"]) for c in range(NCORES)]

    maps = []
    for c in range(NCORES):
        b, sl = tok(c)
        maps.append({"x": np.ascontiguousarray(x0[b, sl]),
                     "pA": np.ascontiguousarray(parts[2 * b][sl]),
                     "pB": np.ascontiguousarray(parts[2 * b + 1][sl]),
                     "W": ins["L0_ffn_weight"], "idn": idn})
    r = run_launch(build_l0ffn(), maps)
    x2 = asm_tok(r, "xo")

    w1 = {k[len("L1_attn_"):]: ins[k] for k in ins if k.startswith("L1_attn_")}
    r = run_launch(build_attn01(1), attn01_in_maps(1, x2, w1))
    parts = [np.asarray(r[c]["part"]) for c in range(NCORES)]

    wg_, wu_, wd_ = (ins["L1_ffn_w_gate"], ins["L1_ffn_w_up"],
                     ins["L1_ffn_w_down"])
    maps = [{"wg": np.ascontiguousarray(wg_[c * 352:(c + 1) * 352]),
             "wu": np.ascontiguousarray(wu_[c * 352:(c + 1) * 352]),
             "wd": np.ascontiguousarray(wd_[c * 128:(c + 1) * 128])}
            for c in range(NCORES)]
    r = run_launch(build_l1fprep(), maps)
    wgq = np.concatenate([np.asarray(r[c]["wgq"]) for c in range(NCORES)])
    wuq = np.concatenate([np.asarray(r[c]["wuq"]) for c in range(NCORES)])
    wdq = np.concatenate([np.asarray(r[c]["wdq"]) for c in range(NCORES)])

    maps = []
    for c in range(NCORES):
        b, sl = tok(c)
        maps.append({"x": np.ascontiguousarray(x2[b, sl]),
                     "pA": np.ascontiguousarray(parts[2 * b][sl]),
                     "pB": np.ascontiguousarray(parts[2 * b + 1][sl]),
                     "wgq": wgq, "wuq": wuq, "wdq": wdq, "idn": idn})
    r = run_launch(build_l1ffn(), maps)
    x4 = asm_tok(r, "xo")

    tri = host_tri_mask()
    w_in = ins["L2_attn_w_in"]
    wc_, bc_ = ins["L2_attn_w_conv"], ins["L2_attn_b_conv"]
    maps = []
    for c in range(NCORES):
        b, hh = c // 2, c % 2
        h0 = hh * 16
        zr = slice(h0 * 64, h0 * 64 + 1024)
        xr = slice(2048 + h0 * 64, 2048 + h0 * 64 + 1024)
        chan = list(range(h0 * 64, h0 * 64 + 1024)) + list(range(2048, 2304))
        wcc = np.zeros((128, 40), np.float32)
        bcc = np.zeros((128, 10), np.float32)
        for f in range(10):
            for pp in range(128):
                ch = chan[f * 128 + pp]
                wcc[pp, f * 4:(f + 1) * 4] = wc_[ch]
                bcc[pp, f] = bc_[ch]
        maps.append({"x": np.ascontiguousarray(x4[b]),
                     "wzT": np.ascontiguousarray(w_in[zr].T),
                     "wxT": np.ascontiguousarray(w_in[xr].T),
                     "wbcT": np.ascontiguousarray(w_in[4096:4352].T),
                     "wdtT": np.ascontiguousarray(w_in[4352 + h0:4352 + h0 + 16].T),
                     "wln1": np.ascontiguousarray(
                         ins["L2_attn_w_ln1"].reshape(8, 128).T),
                     "wcc": wcc, "bcc": bcc,
                     "dtb": ins["L2_attn_dt_bias"][h0:h0 + 16].reshape(16, 1),
                     "alog": ins["L2_attn_A_log"][h0:h0 + 16].reshape(16, 1),
                     "drow": np.ascontiguousarray(np.broadcast_to(
                         ins["L2_attn_D"][h0:h0 + 16][None, :], (128, 16))),
                     "idn": idn, "idnf": idnf, "tri": tri})
    r = run_launch(build_l2a(), maps)
    ygf = [np.concatenate([np.asarray(r[2 * b]["yg"]),
                           np.asarray(r[2 * b + 1]["yg"])], axis=1)
           for b in range(4)]

    maps = []
    for c in range(NCORES):
        b, sl = tok(c)
        maps.append({"x": np.ascontiguousarray(x4[b, sl]),
                     "yg": np.ascontiguousarray(ygf[b][sl]),
                     "woT": np.ascontiguousarray(ins["L2_attn_w_out"].T),
                     "wgT": np.ascontiguousarray(ins["L2_attn_w_gate"].T),
                     "wuT": np.ascontiguousarray(ins["L2_attn_w_up"].T),
                     "wmn": np.ascontiguousarray(
                         ins["L2_attn_w_mnorm"].reshape(16, 128).T),
                     "wl2": np.ascontiguousarray(
                         ins["L2_attn_w_ln2"].reshape(8, 128).T),
                     "idn": idn})
    r1 = run_launch(build_l2b1(), maps)
    maps = []
    for c in range(NCORES):
        b, sl = tok(c)
        maps.append({"hid": np.asarray(r1[c]["hid"]),
                     "hf": np.asarray(r1[c]["hf"]),
                     "x": np.ascontiguousarray(x4[b, sl]),
                     "wdT": np.ascontiguousarray(ins["L2_attn_w_down"].T),
                     "idn": idn})
    r = run_launch(build_l2b2(), maps)
    x5 = asm_tok(r, "xo")

    maps = []
    for c in range(NCORES):
        b, sl = tok(c)
        m = {"x": np.ascontiguousarray(x5[b, sl]),
             "gwT": np.ascontiguousarray(ins["L2_ffn_gate_w"].T),
             "idn": idn, "idnf": idnf}
        for e in range(8):
            m[f"w1T{e}"] = np.ascontiguousarray(ins["L2_ffn_w1"][e].T)
            m[f"w3T{e}"] = np.ascontiguousarray(ins["L2_ffn_w3"][e].T)
            m[f"w2T{e}"] = np.ascontiguousarray(ins["L2_ffn_w2"][e].T)
        maps.append(m)
    r = run_launch(build_l2ffn(), maps)
    x6 = asm_tok(r, "xo")

    bdm = np.full((128, 128), -60000.0, np.float32)
    for c2 in range(2):
        for i in range(64):
            bdm[c2 * 64 + i, c2 * 64 + i:c2 * 64 + 64] = 0.0
    bdm = bdm.astype(BF)
    Wqkv = ins["L3_attn_W_qkv"]
    maps = []
    for c in range(NCORES):
        b, hh = c // 2, c % 2
        cs = hh * 8 * 64
        maps.append({"x": np.ascontiguousarray(x6[b]),
                     "Wq": np.ascontiguousarray(Wqkv[:, cs:cs + 512]),
                     "Wk": np.ascontiguousarray(Wqkv[:, 1024 + cs:1024 + cs + 512]),
                     "Wv": np.ascontiguousarray(Wqkv[:, 2048 + cs:2048 + cs + 512]),
                     "Wo": np.ascontiguousarray(ins["L3_attn_W_out"][cs:cs + 512, :]),
                     "idn": idn, "idnf": idnf, "bdm": bdm})
    r = run_launch(build_l3a(), maps)
    parts = [np.asarray(r[c]["part"]) for c in range(NCORES)]

    wshared = {"qpT": np.ascontiguousarray(ins["L3_ffn_q_proj_w"].T),
               "rwT": np.ascontiguousarray(ins["L3_ffn_router_w"].T),
               "akT": np.ascontiguousarray(ins["L3_ffn_aux_keys"].T),
               "avals": ins["L3_ffn_aux_values"],
               "rel": ins["L3_ffn_reliability_mask"].reshape(1, 4096),
               "uw1T": np.ascontiguousarray(ins["L3_ffn_unc_w1"].T),
               "ub1": np.ascontiguousarray(ins["L3_ffn_unc_b1"].reshape(2, 128).T),
               "uw2T": np.ascontiguousarray(ins["L3_ffn_unc_w2"].T),
               "idn": idn, "idnf": idnf}
    maps = []
    for c in range(NCORES):
        b, sl = tok(c)
        m = dict(wshared)
        m["x"] = np.ascontiguousarray(x6[b, sl])
        m["pA"] = np.ascontiguousarray(parts[2 * b][sl])
        m["pB"] = np.ascontiguousarray(parts[2 * b + 1][sl])
        maps.append(m)
    r1 = run_launch(build_l3fa(), maps)
    lvp_all = np.concatenate([np.asarray(r1[c]["lvp"]) for c in range(NCORES)],
                             axis=1).astype(np.float32)
    w2shared = {"ub2": np.ascontiguousarray(np.broadcast_to(
                    ins["L3_ffn_unc_b2"].reshape(1, 1), (128, 1))),
                "gw1": np.ascontiguousarray(np.broadcast_to(
                    ins["L3_ffn_gate_w1"].reshape(1, 1), (128, 1))),
                "gb": np.ascontiguousarray(np.broadcast_to(
                    ins["L3_ffn_gate_bias"].reshape(1, 1), (128, 1)))}
    maps = []
    for c in range(NCORES):
        m = dict(w2shared)
        m["x"] = np.asarray(r1[c]["x7"])
        m["agg"] = np.asarray(r1[c]["agg"])
        m["lrn"] = np.asarray(r1[c]["lrn"])
        m["lvp"] = lvp_all
        maps.append(m)
    r = run_launch(build_l3fb(), maps)
    return asm_tok(r, "xo")



TS = 2048


def build_l0ffn():
    p = Prog("l0ffn")
    nc = p.nc
    x_in = p.inp("x", [TS, E])
    pA = p.inp("pA", [TS, E])
    pB = p.inp("pB", [TS, E])
    W = p.inp("W", [E, E])
    idn_in = p.inp("idn", [128, 128], BF16)
    x_out = p.outp("xo", [TS, E])

    x1 = p.sb("x1", [128, 16 * E])
    xnT = p.sb("xnT", [128, 8 * 512], BF16)
    xn_tok = p.sb("xn_tok", [128, E], BF16)
    p._lnsq = xn_tok
    stats = p.sb("stats", [128, 16])
    idn = p.sb("idn_s", [128, 128], BF16)
    wn = p.sb("wn", [128, 8 * E], BF16)
    sgT = p.sb("sgT", [128, 8 * E], BF16)
    ptl = p.sb("ptl", [128, 2 * E])
    ob = p.sb("ob", [128, E])
    ld = p.sem("ld")
    cs = p.sem("cs")
    mm = p.sem("mm")
    ev = MSem(p, "ev")
    ps1 = p.ps("ps1", [128, 512])
    ps2 = p.ps("ps2", [128, 512])
    ptb = p.ps("ptb", [128, 128], BF16)
    p.constcol(1e-5); p.constcol(-0.5)

    ld.n += 16; nc.sync.dma_start(out=idn[:], in_=idn_in[:, :]).then_inc(ld.h, 16)
    for kt in range(8):
        cs.n += 16
        nc.gpsimd.dma_start(out=wn[:, kt * E:(kt + 1) * E],
                            in_=dram_slice(W, kt * 128, 128, 0, E)).then_inc(cs.h, 16)
    for e in p.eng_all():
        e.wait_ge(ld.h, ld.n)
        e.wait_ge(cs.h, cs.n)
    p.barrier()
    for ot in range(8):
        nc.scalar.activation(out=wn[:, ot * E:(ot + 1) * E],
                             in_=wn[:, ot * E:(ot + 1) * E], func=AF.Sign)
    p.sig(nc.scalar, ev)
    ev.wait(nc.tensor)
    for ot in range(8):
        for it in range(8):
            nc.tensor.transpose(ptb[:, :],
                                wn[:, ot * E + it * 128: ot * E + (it + 1) * 128],
                                idn[:, :])
            p.sig(nc.tensor, mm)
            nc.vector.wait_ge(mm.h, mm.n)
            nc.vector.tensor_copy(out=sgT[:, it * E + ot * 128: it * E + (ot + 1) * 128],
                                  in_=ptb[:, :])
            p.sig(nc.vector, ev)
            ev.wait(nc.tensor)
    p.barrier()
    for ti in range(16):
        xv = x1[:, ti * E:(ti + 1) * E]
        ld.n += 16
        nc.sync.dma_start(out=xv, in_=x_in[ti * 128:(ti + 1) * 128, :]).then_inc(ld.h, 16)
        ld.n += 16
        nc.sync.dma_start(out=ptl[:, 0:E], in_=pA[ti * 128:(ti + 1) * 128, :]).then_inc(ld.h, 16)
        ld.n += 16
        nc.sync.dma_start(out=ptl[:, E:2 * E], in_=pB[ti * 128:(ti + 1) * 128, :]).then_inc(ld.h, 16)
        nc.vector.wait_ge(ld.h, ld.n)
        nc.vector.tensor_add(out=xv, in0=xv, in1=ptl[:, 0:E])
        p.chain(nc.vector)
        nc.vector.tensor_add(out=xv, in0=xv, in1=ptl[:, E:2 * E])
        p.sig(nc.vector, ev)
        ev.wait(nc.sync)
    p.barrier()
    for g in range(4):
        for tt in range(4):
            ti = g * 4 + tt
            xv = x1[:, ti * E:(ti + 1) * E]
            nc.scalar.wait_ge(mm.h, mm.n)
            bcol, scol = emit_ln_stats(p, xv, stats, 0, E, 1e-5)
            nc.vector.wait_ge(mm.h, mm.n)
            nc.vector.tensor_scalar(out=xn_tok[:, :], in0=xv, scalar1=scol,
                                    scalar2=bcol, op0=ALU.mult, op1=ALU.add)
            p.sig(nc.vector, ev)
            ev.wait(nc.tensor)
            for kt in range(8):
                nc.tensor.transpose(ptb[:, :], xn_tok[:, kt * 128:(kt + 1) * 128],
                                    idn[:, :])
                p.sig(nc.tensor, mm)
                nc.vector.wait_ge(mm.h, mm.n)
                nc.vector.tensor_copy(
                    out=xnT[:, kt * 512 + tt * 128: kt * 512 + (tt + 1) * 128],
                    in_=ptb[:, :])
                p.sig(nc.vector, ev)
                ev.wait(nc.tensor)
        p.barrier([nc.tensor, nc.vector, nc.scalar, nc.sync])
        for tt in range(4):
            ti = g * 4 + tt
            for oc in range(2):
                pso = ps1 if oc == 0 else ps2
                for kt in range(8):
                    nc.tensor.matmul(pso[:, :],
                                     xnT[:, kt * 512 + tt * 128: kt * 512 + (tt + 1) * 128],
                                     sgT[:, kt * E + oc * 512: kt * E + (oc + 1) * 512],
                                     start=(kt == 0), stop=(kt == 7))
            p.sig(nc.tensor, mm)
            nc.vector.wait_ge(mm.h, mm.n)
            nc.vector.wait_ge(ld.h, ld.n)
            nc.vector.tensor_add(out=ob[:, 0:512], in0=ps1[:, :],
                                 in1=x1[:, ti * E: ti * E + 512])
            nc.vector.tensor_add(out=ob[:, 512:1024], in0=ps2[:, :],
                                 in1=x1[:, ti * E + 512: (ti + 1) * E])
            p.sig(nc.vector, ev)
            ev.wait(nc.sync)
            ev.wait(nc.tensor)
            ld.n += 16
            nc.sync.dma_start(out=x_out[ti * 128:(ti + 1) * 128, :],
                              in_=ob[:, :]).then_inc(ld.h, 16)
    nc.sync.wait_ge(ld.h, ld.n)
    p.barrier()
    return p



def emit_fp4(p, xv, out_bf, C, scr, nr=128):
    nc = p.nc
    G = C // 16
    t, w1, w2 = (scr[k][0:nr, 0:C] for k in ("t", "w1", "w2"))
    gm = scr["gm"]
    gmax = gm[0:nr, 0:G]
    inv = gm[0:nr, G:2 * G]
    g3 = scr["g3"][0:nr, 0:C]
    ev = scr["ev"]
    p.sig(nc.vector, ev)
    ev.wait(nc.scalar)
    xg = xv.rearrange("p (g k) -> p g k", k=16)
    nc.vector.tensor_reduce(out=gmax, in_=xg, axis=AX.X, op=ALU.max,
                            apply_absolute_value=True)
    p.chain(nc.vector)
    nc.vector.tensor_scalar(out=gmax, in0=gmax, scalar1=1.0 / 6.0,
                            scalar2=1e-12, op0=ALU.mult, op1=ALU.max)
    p.chain(nc.vector)
    nc.vector.reciprocal(out=inv, in_=gmax)
    nc.scalar.activation(out=t, in_=xv, func=AF.Abs)
    p.sig(nc.scalar, ev); p.sig(nc.vector, ev)
    ev.wait(nc.vector)
    invb = inv.rearrange("p (g o) -> p g o", o=1).broadcast_to((nr, G, 16))
    nc.vector.tensor_tensor(out=t.rearrange("p (g k) -> p g k", k=16),
                            in0=t.rearrange("p (g k) -> p g k", k=16),
                            in1=invb, op=ALU.mult)
    p.chain(nc.vector)
    nc.vector.tensor_single_scalar(out=w1, in_=t, scalar=0.25, op=ALU.is_gt)
    nc.vector.tensor_single_scalar(out=w2, in_=t, scalar=0.75, op=ALU.is_gt)
    p.chain(nc.vector)
    nc.vector.tensor_add(out=w1, in0=w1, in1=w2)
    p.chain(nc.vector)
    nc.vector.tensor_scalar(out=w1, in0=w1, scalar1=0.5, scalar2=None,
                            op0=ALU.mult)
    ti32 = t.bitcast(I32)
    w2i = w2.bitcast(I32)
    p.chain(nc.vector)
    nc.vector.tensor_single_scalar(out=w2i, in_=ti32, scalar=0x1FFFFF, op=ALU.add)
    p.chain(nc.vector)
    nc.vector.tensor_single_scalar(out=w2i, in_=w2i, scalar=-4194304,
                                   op=ALU.bitwise_and)
    nc.vector.tensor_single_scalar(out=g3, in_=t, scalar=1.25, op=ALU.is_gt)
    p.chain(nc.vector)
    nc.vector.copy_predicated(out=w1, mask=g3, data=w2)
    p.sig(nc.vector, ev)
    ev.wait(nc.scalar)
    nc.scalar.activation(out=t, in_=xv, func=AF.Sign)
    p.sig(nc.scalar, ev); p.sig(nc.vector, ev)
    ev.wait(nc.vector)
    nc.vector.tensor_mul(out=w1, in0=w1, in1=t)
    p.chain(nc.vector)
    sb_ = gmax.rearrange("p (g o) -> p g o", o=1).broadcast_to((nr, G, 16))
    nc.vector.tensor_tensor(out=out_bf.rearrange("p (g k) -> p g k", k=16),
                            in0=w1.rearrange("p (g k) -> p g k", k=16),
                            in1=sb_, op=ALU.mult)
    p.sig(nc.vector, ev)



def build_l1fprep():
    p = Prog("l1fprep")
    nc = p.nc
    wg = p.inp("wg", [352, 1024])
    wu = p.inp("wu", [352, 1024])
    wd = p.inp("wd", [128, 2816])
    wgq = p.outp("wgq", [352, 1024], BF16)
    wuq = p.outp("wuq", [352, 1024], BF16)
    wdq = p.outp("wdq", [128, 2816], BF16)
    C = 2816
    xv = p.sb("xv", [128, C])
    t = p.sb("t", [128, C]); w1 = p.sb("w1", [128, C])
    w2 = p.sb("w2", [128, C])
    gm = p.sb("gm", [128, 2 * (C // 16)])
    g3 = p.sb("g3", [128, C], U8)
    ob = p.sb("ob", [128, C], BF16)
    ld = p.sem("ld")
    st = p.sem("st")
    ev = MSem(p, "ev")
    scr = {"t": t, "w1": w1, "w2": w2, "gm": gm, "g3": g3, "ev": ev}
    jobs = [(wg, wgq, 1024, [128, 128, 96]), (wu, wuq, 1024, [128, 128, 96]),
            (wd, wdq, 2816, [128])]
    for (src, dst, Cw, rows) in jobs:
        r0 = 0
        for nr in rows:
            ld.n += 16
            nc.sync.dma_start(out=xv[0:nr, 0:Cw], in_=src[r0:r0 + nr, :]).then_inc(ld.h, 16)
            nc.vector.wait_ge(ld.h, ld.n)
            nc.scalar.wait_ge(ld.h, ld.n)
            emit_fp4(p, xv[0:nr, 0:Cw], ob[0:nr, 0:Cw], Cw, scr, nr=nr)
            ev.wait(nc.sync)
            st.n += 16
            nc.sync.dma_start(out=dst[r0:r0 + nr, :], in_=ob[0:nr, 0:Cw]).then_inc(st.h, 16)
            nc.vector.wait_ge(st.h, st.n)
            nc.scalar.wait_ge(st.h, st.n)
            r0 += nr
    nc.sync.wait_ge(st.h, st.n)
    p.barrier()
    return p



def build_l1ffn():
    p = Prog("l1ffn")
    nc = p.nc
    I = 2816
    x_in = p.inp("x", [TS, E])
    pA = p.inp("pA", [TS, E])
    pB = p.inp("pB", [TS, E])
    wgq = p.inp("wgq", [I, E], BF16)
    wuq = p.inp("wuq", [I, E], BF16)
    wdq = p.inp("wdq", [E, I], BF16)
    idn_in = p.inp("idn", [128, 128], BF16)
    x_out = p.outp("xo", [TS, E])

    wgT = p.sb("wgT", [128, 8 * I], BF16)
    wuT = p.sb("wuT", [128, 8 * I], BF16)
    wdT = p.sb("wdT", [128, 22 * E], BF16)
    xn = p.sb("xn", [128, E])
    x3 = p.sb("x3", [128, E])
    t = p.sb("t", [128, I]); w1 = p.sb("w1", [128, I]); w2 = p.sb("w2", [128, I])
    gm = p.sb("gm", [128, 2 * (I // 16)])
    g3 = p.sb("g3", [128, I], U8)
    h_tok = p.sb("h_tok", [128, I])
    hq = p.sb("hq", [128, I], BF16)
    hqT = p.sb("hqT", [128, 22 * 128], BF16)
    ob = p.sb("ob", [128, E])
    sil = ob[:, 0:512]
    idn = p.sb("idn_s", [128, 128], BF16)
    stats = p.sb("stats", [128, 16])
    p._lnsq = t
    ld = p.sem("ld")
    st = p.sem("st")
    mm = p.sem("mm")
    ev = MSem(p, "ev")
    scr = {"t": t, "w1": w1, "w2": w2, "gm": gm, "g3": g3, "ev": ev}
    ps_g = p.ps("ps_g", [128, 512])
    ps_u = p.ps("ps_u", [128, 512])
    ps_d1 = p.ps("ps_d1", [128, 512])
    ps_d2 = p.ps("ps_d2", [128, 512])
    ptb = p.ps("ptb", [128, 128], BF16)
    p.constcol(1e-5); p.constcol(-0.5)

    ld.n += 16; nc.sync.dma_start(out=idn[:], in_=idn_in[:, :]).then_inc(ld.h, 16)
    for it in range(8):
        for (wsrc, wdst) in ((wgq, wgT), (wuq, wuT)):
            ld.n += 16
            nc.sync.dma_start(out=wdst[:, it * I:(it + 1) * I],
                              in_=wsrc[:, it * 128:(it + 1) * 128],
                              transpose=True).then_inc(ld.h, 16)
    for it in range(22):
        ld.n += 16
        nc.sync.dma_start(out=wdT[:, it * E:(it + 1) * E],
                          in_=wdq[:, it * 128:(it + 1) * 128],
                          transpose=True).then_inc(ld.h, 16)
    for e in p.eng_all():
        e.wait_ge(ld.h, ld.n)
    p.barrier()
    OCS = [(0, 512), (512, 512), (1024, 512), (1536, 512), (2048, 512), (2560, 256)]
    for ti in range(16):
        ld.n += 16
        nc.sync.dma_start(out=x3[:, :], in_=x_in[ti * 128:(ti + 1) * 128, :]).then_inc(ld.h, 16)
        ld.n += 16
        nc.sync.dma_start(out=w1[:, 0:E], in_=pA[ti * 128:(ti + 1) * 128, :]).then_inc(ld.h, 16)
        ld.n += 16
        nc.sync.dma_start(out=w2[:, 0:E], in_=pB[ti * 128:(ti + 1) * 128, :]).then_inc(ld.h, 16)
        nc.vector.wait_ge(ld.h, ld.n)
        nc.vector.wait_ge(st.h, st.n)
        nc.vector.wait_ge(mm.h, mm.n)
        nc.vector.tensor_add(out=x3[:, :], in0=x3[:, :], in1=w1[:, 0:E])
        p.chain(nc.vector)
        nc.vector.tensor_add(out=x3[:, :], in0=x3[:, :], in1=w2[:, 0:E])
        p.sig(nc.vector, ev)
        ev.wait(nc.scalar)
        bcol, scol = emit_ln_stats(p, x3[:, :], stats, 0, E, 1e-5)
        nc.vector.tensor_scalar(out=xn[:, :], in0=x3[:, :], scalar1=scol,
                                scalar2=bcol, op0=ALU.mult, op1=ALU.add)
        p.sig(nc.vector, ev)
        ev.wait(nc.scalar)
        emit_fp4(p, xn[:, :], hq[:, 0:E], E, scr)
        ev.wait(nc.tensor)
        for kt in range(8):
            nc.tensor.transpose(ptb[:, :], hq[:, kt * 128:(kt + 1) * 128], idn[:, :])
            p.sig(nc.tensor, mm)
            nc.vector.wait_ge(mm.h, mm.n)
            nc.vector.tensor_copy(out=hqT[:, kt * 128:(kt + 1) * 128], in_=ptb[:, :])
            p.sig(nc.vector, ev)
            ev.wait(nc.tensor)
        nc.scalar.wait_ge(st.h, st.n)
        for (o0, cw) in OCS:
            for kt in range(8):
                nc.tensor.matmul(ps_g[:, 0:cw], hqT[:, kt * 128:(kt + 1) * 128],
                                 wgT[:, kt * I + o0: kt * I + o0 + cw],
                                 start=(kt == 0), stop=(kt == 7))
            for kt in range(8):
                nc.tensor.matmul(ps_u[:, 0:cw], hqT[:, kt * 128:(kt + 1) * 128],
                                 wuT[:, kt * I + o0: kt * I + o0 + cw],
                                 start=(kt == 0), stop=(kt == 7))
            p.sig(nc.tensor, mm)
            nc.scalar.wait_ge(mm.h, mm.n)
            nc.scalar.activation(out=sil[0:128, 0:cw], in_=ps_g[:, 0:cw], func=AF.Silu)
            p.sig(nc.scalar, ev)
            ev.wait(nc.vector)
            nc.vector.tensor_tensor(out=h_tok[:, o0:o0 + cw], in0=sil[0:128, 0:cw],
                                    in1=ps_u[:, 0:cw], op=ALU.mult)
            p.sig(nc.vector, ev)
            ev.wait(nc.tensor)
        emit_fp4(p, h_tok[:, :], hq[:, :], I, scr)
        ev.wait(nc.tensor)
        for it in range(22):
            nc.tensor.transpose(ptb[:, :], hq[:, it * 128:(it + 1) * 128], idn[:, :])
            p.sig(nc.tensor, mm)
            nc.vector.wait_ge(mm.h, mm.n)
            nc.vector.tensor_copy(out=hqT[:, it * 128:(it + 1) * 128], in_=ptb[:, :])
            p.sig(nc.vector, ev)
            ev.wait(nc.tensor)
        for oc2 in range(2):
            psd = ps_d1 if oc2 == 0 else ps_d2
            for it in range(22):
                nc.tensor.matmul(psd[:, :], hqT[:, it * 128:(it + 1) * 128],
                                 wdT[:, it * E + oc2 * 512: it * E + (oc2 + 1) * 512],
                                 start=(it == 0), stop=(it == 21))
        p.sig(nc.tensor, mm)
        nc.vector.wait_ge(mm.h, mm.n)
        nc.vector.wait_ge(st.h, st.n)
        nc.vector.tensor_add(out=ob[:, 0:512], in0=ps_d1[:, :], in1=x3[:, 0:512])
        nc.vector.tensor_add(out=ob[:, 512:1024], in0=ps_d2[:, :], in1=x3[:, 512:1024])
        p.sig(nc.vector, ev)
        ev.wait(nc.sync)
        st.n += 16
        nc.sync.dma_start(out=x_out[ti * 128:(ti + 1) * 128, :],
                          in_=ob[:, :]).then_inc(st.h, 16)
    nc.sync.wait_ge(st.h, st.n)
    p.barrier()
    return p



def build_l2a():
    p = Prog("l2a")
    nc = p.nc
    NH = 16
    P_ = 64
    N = 128
    x_in = p.inp("x", [S, E])
    wzT = p.inp("wzT", [E, 1024])
    wxT = p.inp("wxT", [E, 1024])
    wbcT = p.inp("wbcT", [E, 256])
    wdtT = p.inp("wdtT", [E, 16])
    wln1 = p.inp("wln1", [128, 8])
    wcc = p.inp("wcc", [128, 40])
    bcc = p.inp("bcc", [128, 10])
    dtb = p.inp("dtb", [16, 1])
    alog = p.inp("alog", [16, 1])
    drow_in = p.inp("drow", [128, 16])
    idn_in = p.inp("idn", [128, 128], BF16)
    idnf_in = p.inp("idnf", [128, 128], F32)
    tri_in = p.inp("tri", [128, 128], BF16)
    yg_out = p.outp("yg", [S, 1024])

    wz_s = p.sb("wz_s", [128, 8 * 1024], BF16)
    wx_s = p.sb("wx_s", [128, 8 * 1024], BF16)
    wbc_s = p.sb("wbc_s", [128, 8 * 256], BF16)
    wdt_s = p.sb("wdt_s", [128, 8 * 16], BF16)
    wln1_s = p.sb("wln1_s", [128, 8])
    wcc_s = p.sb("wcc_s", [128, 40])
    bcc_s = p.sb("bcc_s", [128, 10])
    dtb_s = p.sb("dtb_s", [16, 1])
    acol = p.sb("acol", [16, 1])
    drow_s = p.sb("drow_s", [128, 16])
    idn = p.sb("idn_s", [128, 128], BF16)
    idnf = p.sb("idnf_s", [128, 128], F32)
    tri = p.sb("tri_s", [128, 128], BF16)
    xck = p.sb("xck", [128, E])
    h1n = p.sb("h1n", [128, E], BF16)
    p._lnsq = p.sb("lnsq2", [128, E])
    h1T = p.sb("h1T", [128, 8 * 128], BF16)
    zT = p.sb("zT", [128, 8 * 128], BF16)
    cb = p.sb("cb", [128, 10 * 131])
    xs = p.sb("xs", [128, 10 * 128], BF16)
    xtok = p.sb("xtok", [128, 10 * 128], BF16)
    XD = p.sb("XD", [128, NH * P_], BF16)
    dtw = p.sb("dtw", [16, 5 * 128])
    ones16 = p.sb("ones16", [16, 128])
    csT = p.sb("csT", [128, 16])
    ncsT = p.sb("ncsT", [128, 16])
    EcsT = p.sb("EcsT", [128, 16])
    dtT = p.sb("dtT", [128, 16])
    csrow = p.sb("csrow", [1, 128])
    earow = p.sb("earow", [1, 16])
    Etile = p.sb("Etile", [128, 128])
    eacol = p.sb("eacol", [128, 1])
    WT = p.sb("WT", [128, 128], BF16)
    Bd = p.sb("Bd", [128, 128], BF16)
    carry = p.sb("carry", [128, NH * P_])
    ytile = p.sb("ytile", [128, NH * P_])
    yoff = p.sb("yoff", [128, P_])
    sz = p.sb("sz", [128, 1024], BF16)
    obuf = p.sb("obuf", [128, 1024])
    stats = p.sb("stats", [128, 16])
    ld = p.sem("ld")
    cs_sem = p.sem("cs")
    st = p.sem("st")
    mm = p.sem("mm")
    ev = MSem(p, "ev")
    ps_pj = p.ps("ps_pj", [128, 128])
    ps_G = p.ps("ps_G", [128, 128])
    ps_D = p.ps("ps_D", [128, 128])
    ps_Y = p.ps("ps_Y", [128, P_])
    ps_O = p.ps("ps_O", [128, P_])
    ps_S = p.ps("ps_S", [128, P_])
    ptb = p.ps("ptb", [128, 128], BF16)
    ptf = p.ps("ptf", [128, 128], F32)
    p.constcol(1e-5); p.constcol(-0.5); p.constcol(1e-6); p.constcol(-1.0)

    for (w, ws, nf) in ((wzT, wz_s, 1024), (wxT, wx_s, 1024),
                        (wbcT, wbc_s, 256), (wdtT, wdt_s, 16)):
        for kt in range(8):
            cs_sem.n += 16
            nc.gpsimd.dma_start(out=ws[:, kt * nf:(kt + 1) * nf],
                                in_=dram_slice(w, kt * 128, 128, 0, nf)
                                ).then_inc(cs_sem.h, 16)
    for (src, dst) in ((wln1, wln1_s), (wcc, wcc_s), (bcc, bcc_s),
                       (dtb, dtb_s), (alog, acol), (drow_in, drow_s),
                       (idn_in, idn), (idnf_in, idnf), (tri_in, tri)):
        ld.n += 16
        nc.sync.dma_start(out=dst[:, :] if dst.shape[0] == 128 else dst[0:dst.shape[0], :],
                          in_=src[:, :]).then_inc(ld.h, 16)
    for e in p.eng_all():
        e.wait_ge(ld.h, ld.n)
        e.wait_ge(cs_sem.h, cs_sem.n)
    p.barrier()
    nc.vector.memset(ones16[0:16, :], 1.0)
    nc.vector.memset(carry[:, :], 0.0)
    nc.vector.memset(cb[:, :], 0.0)
    _onesrow_l2(p)
    nc.scalar.activation(out=acol[0:16, :], in_=acol[0:16, :], func=AF.Exp)
    p.sig(nc.scalar, ev)
    ev.wait(nc.vector)
    nc.vector.tensor_scalar(out=acol[0:16, :], in0=acol[0:16, :], scalar1=-1.0,
                            scalar2=None, op0=ALU.mult)
    p.sig(nc.vector, ev)
    ev.wait(nc.vector)
    for kt in range(8):
        for (ws, nf) in ((wz_s, 1024), (wx_s, 1024), (wbc_s, 256), (wdt_s, 16)):
            nc.vector.tensor_scalar(out=ws[:, kt * nf:(kt + 1) * nf],
                                    in0=ws[:, kt * nf:(kt + 1) * nf],
                                    scalar1=wln1_s[:, kt:kt + 1], scalar2=None,
                                    op0=ALU.mult)
    p.barrier()

    for c in range(32):
        t0 = c * 128
        ld.n += 16
        nc.sync.dma_start(out=xck[:, :], in_=x_in[t0:t0 + 128, :]).then_inc(ld.h, 16)
        nc.vector.wait_ge(ld.h, ld.n)
        nc.scalar.wait_ge(ld.h, ld.n)
        nc.scalar.wait_ge(mm.h, mm.n)
        ev.wait(nc.scalar)
        bcol, scol = emit_ln_stats(p, xck[:, :], stats, 0, E, 1e-5)
        c0 = stats[:, 0:1]; c1 = stats[:, 1:2]; c2 = stats[:, 2:3]; c3 = stats[:, 3:4]
        c4 = stats[:, 4:5]; c5 = stats[:, 5:6]
        nc.vector.tensor_mul(out=c4, in0=c3, in1=c3)
        p.chain(nc.vector)
        nc.vector.tensor_mul(out=c4, in0=c4, in1=c1)
        p.sig(nc.vector, ev)
        ev.wait(nc.scalar)
        nc.scalar.activation(out=c4, in_=c4, func=AF.Ln, bias=p.constcol(1e-6)[:, 0:1])
        p.sig(nc.scalar, ev)
        p.chain(nc.scalar)
        nc.scalar.activation(out=c4, in_=c4, func=AF.Exp,
                             scale=p.constcol(-0.5)[:, 0:1])
        p.sig(nc.scalar, ev)
        ev.wait(nc.vector)
        nc.vector.tensor_mul(out=c4, in0=c4, in1=c3)
        p.chain(nc.vector)
        nc.vector.tensor_mul(out=c5, in0=c4, in1=c0)
        p.chain(nc.vector)
        nc.vector.tensor_scalar(out=c5, in0=c5, scalar1=-1.0, scalar2=None,
                                op0=ALU.mult)
        p.chain(nc.vector)
        nc.vector.wait_ge(mm.h, mm.n)
        nc.vector.tensor_scalar(out=h1n[:, :], in0=xck[:, :], scalar1=c4,
                                scalar2=c5, op0=ALU.mult, op1=ALU.add)
        p.sig(nc.vector, ev)
        ev.wait(nc.tensor)
        for kt in range(8):
            nc.tensor.transpose(ptb[:, :], h1n[:, kt * 128:(kt + 1) * 128], idn[:, :])
            p.sig(nc.tensor, mm)
            nc.vector.wait_ge(mm.h, mm.n)
            nc.vector.tensor_copy(out=h1T[:, kt * 128:(kt + 1) * 128], in_=ptb[:, :])
            p.sig(nc.vector, ev)
            ev.wait(nc.tensor)
        segs = [(wz_s, 1024, "z"), (wx_s, 1024, "x"), (wbc_s, 256, "bc"),
                (wdt_s, 16, "dt")]
        for (ws, nf, tag) in segs:
            nft = (nf + 127) // 128
            for f in range(nft):
                fw = min(128, nf - f * 128)
                for kt in range(8):
                    nc.tensor.matmul(ps_pj[0:fw, :],
                                     ws[:, kt * nf + f * 128: kt * nf + f * 128 + fw],
                                     h1T[:, kt * 128:(kt + 1) * 128],
                                     start=(kt == 0), stop=(kt == 7))
                p.sig(nc.tensor, mm)
                nc.vector.wait_ge(mm.h, mm.n)
                if tag == "z":
                    nc.vector.tensor_copy(out=zT[:, f * 128:(f + 1) * 128],
                                          in_=ps_pj[:, :])
                elif tag == "x":
                    nc.vector.tensor_copy(
                        out=cb[:, f * 131 + 3: f * 131 + 131], in_=ps_pj[:, :])
                elif tag == "bc":
                    nc.vector.tensor_copy(
                        out=cb[:, (8 + f) * 131 + 3: (8 + f) * 131 + 131],
                        in_=ps_pj[:, :])
                else:
                    nc.vector.tensor_copy(out=dtw[0:16, 0:128], in_=ps_pj[0:16, :])
                p.sig(nc.vector, ev)
                ev.wait(nc.tensor)
        ev.wait(nc.vector)
        for f in range(10):
            base = f * 131
            nc.vector.tensor_scalar(out=p._lnsq[:, 0:128], in0=cb[:, base:base + 128],
                                    scalar1=wcc_s[:, f * 4:f * 4 + 1], scalar2=None,
                                    op0=ALU.mult)
            p.chain(nc.vector)
            for k in range(1, 4):
                nc.vector.scalar_tensor_tensor(
                    out=p._lnsq[:, 0:128], in0=cb[:, base + k:base + k + 128],
                    scalar=wcc_s[:, f * 4 + k:f * 4 + k + 1],
                    in1=p._lnsq[:, 0:128], op0=ALU.mult, op1=ALU.add)
                p.chain(nc.vector)
            nc.vector.tensor_scalar(out=p._lnsq[:, 0:128], in0=p._lnsq[:, 0:128],
                                    scalar1=bcc_s[:, f:f + 1], scalar2=None,
                                    op0=ALU.add)
            p.sig(nc.vector, ev)
            ev.wait(nc.scalar)
            nc.scalar.activation(out=xs[:, f * 128:(f + 1) * 128],
                                 in_=p._lnsq[:, 0:128], func=AF.Silu)
            p.sig(nc.scalar, ev)
            ev.wait(nc.vector)
            nc.vector.tensor_copy(out=cb[:, base:base + 3],
                                  in_=cb[:, base + 128:base + 131])
            p.sig(nc.vector, ev)
        d0 = dtw[0:16, 0:128]; d1 = dtw[0:16, 128:256]; d2 = dtw[0:16, 256:384]
        d3 = dtw[0:16, 384:512]
        nc.vector.tensor_scalar(out=d0, in0=d0, scalar1=dtb_s[0:16, 0:1],
                                scalar2=None, op0=ALU.add)
        p.sig(nc.vector, ev)
        ev.wait(nc.scalar)
        nc.scalar.activation(out=d1, in_=d0, func=AF.Abs)
        p.chain(nc.scalar)
        nc.scalar.activation(out=d1, in_=d1, func=AF.Exp,
                             scale=p.constcol(-1.0)[0:16, 0:1])
        p.chain(nc.scalar)
        nc.scalar.activation(out=d1, in_=d1, func=AF.Ln, bias=1.0)
        p.sig(nc.scalar, ev)
        ev.wait(nc.vector)
        nc.vector.tensor_relu(out=d2, in_=d0)
        p.chain(nc.vector)
        nc.vector.tensor_add(out=d0, in0=d1, in1=d2)
        p.chain(nc.vector)
        nc.vector.tensor_scalar(out=d2, in0=d0, scalar1=acol[0:16, 0:1],
                                scalar2=None, op0=ALU.mult)
        p.chain(nc.vector)
        nc.vector.tensor_tensor_scan(out=d3, data0=ones16[0:16, :], data1=d2,
                                     initial=0.0, op0=ALU.mult, op1=ALU.add)
        p.sig(nc.vector, ev)
        ev.wait(nc.tensor)
        nc.tensor.transpose(ptf[0:128, 0:16], d3, idnf[0:16, 0:16])
        p.sig(nc.tensor, mm)
        nc.vector.wait_ge(mm.h, mm.n)
        nc.vector.tensor_copy(out=csT[:, :], in_=ptf[0:128, 0:16])
        p.sig(nc.vector, ev)
        ev.wait(nc.tensor)
        nc.tensor.transpose(ptf[0:128, 0:16], d0, idnf[0:16, 0:16])
        p.sig(nc.tensor, mm)
        nc.vector.wait_ge(mm.h, mm.n)
        nc.vector.tensor_copy(out=dtT[:, :], in_=ptf[0:128, 0:16])
        p.chain(nc.vector)
        nc.vector.tensor_scalar(out=ncsT[:, :], in0=csT[:, :], scalar1=-1.0,
                                scalar2=None, op0=ALU.mult)
        p.sig(nc.vector, ev)
        ev.wait(nc.scalar)
        nc.scalar.activation(out=EcsT[:, :], in_=csT[:, :], func=AF.Exp)
        p.sig(nc.scalar, ev)
        ev.wait(nc.tensor)
        for f in range(10):
            nc.tensor.transpose(ptb[:, :], xs[:, f * 128:(f + 1) * 128], idn[:, :])
            p.sig(nc.tensor, mm)
            nc.vector.wait_ge(mm.h, mm.n)
            nc.vector.tensor_copy(out=xtok[:, f * 128:(f + 1) * 128], in_=ptb[:, :])
            p.sig(nc.vector, ev)
            ev.wait(nc.tensor)
        for h in range(NH):
            nc.vector.tensor_scalar(out=XD[:, h * P_:(h + 1) * P_],
                                    in0=xtok[:, h * P_:(h + 1) * P_],
                                    scalar1=dtT[:, h:h + 1], scalar2=None,
                                    op0=ALU.mult)
        p.sig(nc.vector, ev)
        BT = xs[:, 8 * 128:9 * 128]
        CT = xs[:, 9 * 128:10 * 128]
        Btok = xtok[:, 8 * 128:9 * 128]
        nc.tensor.matmul(ps_G[:, :], BT, CT, start=True, stop=True)
        p.sig(nc.tensor, mm)
        for h in range(NH):
            nc.tensor.matmul(ps_D[0:1, :], csT[:, h:h + 1], idnf[:, :],
                             start=True, stop=True)
            p.sig(nc.tensor, mm)
            nc.vector.wait_ge(mm.h, mm.n)
            nc.vector.tensor_copy(out=csrow[0:1, :], in_=ps_D[0:1, 0:128])
            p.sig(nc.vector, ev)
            ev.wait(nc.scalar)
            nc.scalar.activation(out=earow[0:1, h:h + 1],
                                 in_=csrow[0:1, 127:128], func=AF.Exp)
            p.sig(nc.scalar, ev)
            ev.wait(nc.tensor)
            nc.tensor.matmul(ps_D[:, :], _onesrow_l2(p), csrow[0:1, :],
                             start=True, stop=False)
            nc.tensor.matmul(ps_D[:, :], idn[:, :], tri[:, :], start=False,
                             stop=True)
            p.sig(nc.tensor, mm)
            nc.scalar.wait_ge(mm.h, mm.n)
            nc.scalar.activation(out=Etile[:, :], in_=ps_D[:, :], func=AF.Exp,
                                 bias=ncsT[:, h:h + 1])
            p.sig(nc.scalar, ev)
            ev.wait(nc.vector)
            nc.vector.tensor_tensor(out=WT[:, :], in0=Etile[:, :], in1=ps_G[:, :],
                                    op=ALU.mult)
            p.sig(nc.vector, ev)
            ev.wait(nc.tensor)
            nc.tensor.matmul(ps_Y[:, :], WT[:, :], XD[:, h * P_:(h + 1) * P_],
                             start=True, stop=True)
            nc.tensor.matmul(ps_O[:, :], CT, carry[:, h * P_:(h + 1) * P_].bitcast(F32) if False else _carry_bf(p, nc, carry, h, P_, ev),
                             start=True, stop=True)
            p.sig(nc.tensor, mm)
            nc.vector.wait_ge(mm.h, mm.n)
            nc.vector.tensor_scalar(out=yoff[:, :], in0=ps_O[:, :],
                                    scalar1=EcsT[:, h:h + 1], scalar2=None,
                                    op0=ALU.mult)
            p.chain(nc.vector)
            nc.vector.tensor_add(out=yoff[:, :], in0=yoff[:, :], in1=ps_Y[:, :])
            p.chain(nc.vector)
            nc.vector.scalar_tensor_tensor(out=ytile[:, h * P_:(h + 1) * P_],
                                           in0=xtok[:, h * P_:(h + 1) * P_],
                                           scalar=drow_s[:, h:h + 1],
                                           in1=yoff[:, :], op0=ALU.mult,
                                           op1=ALU.add)
            p.sig(nc.vector, ev)
            nc.vector.tensor_scalar(out=Bd[:, :], in0=Btok,
                                    scalar1=Etile[:, 127:128], scalar2=None,
                                    op0=ALU.mult)
            p.sig(nc.vector, ev)
            ev.wait(nc.tensor)
            nc.tensor.matmul(ps_S[:, :], Bd[:, :], XD[:, h * P_:(h + 1) * P_],
                             start=True, stop=True)
            nc.tensor.matmul(ps_D[0:128, 0:1], _onesrow_l2(p), earow[0:1, h:h + 1],
                             start=True, stop=True)
            p.sig(nc.tensor, mm)
            nc.vector.wait_ge(mm.h, mm.n)
            nc.vector.tensor_copy(out=eacol[:, :], in_=ps_D[0:128, 0:1])
            p.chain(nc.vector)
            nc.vector.tensor_scalar(out=carry[:, h * P_:(h + 1) * P_],
                                    in0=carry[:, h * P_:(h + 1) * P_],
                                    scalar1=eacol[:, 0:1], scalar2=None,
                                    op0=ALU.mult)
            p.chain(nc.vector)
            nc.vector.tensor_add(out=carry[:, h * P_:(h + 1) * P_],
                                 in0=carry[:, h * P_:(h + 1) * P_],
                                 in1=ps_S[:, :])
            p.sig(nc.vector, ev)
            ev.wait(nc.tensor)
        for f in range(8):
            nc.tensor.transpose(ptb[:, :], zT[:, f * 128:(f + 1) * 128], idn[:, :])
            p.sig(nc.tensor, mm)
            nc.scalar.wait_ge(mm.h, mm.n)
            nc.scalar.activation(out=sz[:, f * 128:(f + 1) * 128], in_=ptb[:, :],
                                 func=AF.Silu)
            p.sig(nc.scalar, ev)
            ev.wait(nc.tensor)
        ev.wait(nc.vector)
        nc.vector.wait_ge(st.h, st.n)
        nc.vector.tensor_tensor(out=obuf[:, :], in0=ytile[:, :], in1=sz[:, 0:1024],
                                op=ALU.mult)
        p.sig(nc.vector, ev)
        ev.wait(nc.sync)
        st.n += 16
        nc.sync.dma_start(out=yg_out[t0:t0 + 128, :], in_=obuf[:, :]).then_inc(st.h, 16)
    nc.sync.wait_ge(st.h, st.n)
    p.barrier()
    return p


def _onesrow_l2(p):
    key = (id(p), "onesrow_l2")
    if key not in _helpers_cache:
        z = p.sb("onesrow_l2", [1, 128], F32)
        p.nc.vector.memset(z[:, :], 1.0)
        _helpers_cache[key] = z
    return _helpers_cache[key][0:1, :]


def _carry_bf(p, nc, carry, h, P_, ev):
    key = (id(p), "carrybf")
    if key not in _helpers_cache:
        _helpers_cache[key] = p.sb("carrybf", [128, P_], BF16)
    t = _helpers_cache[key]
    nc.vector.tensor_copy(out=t[:, :], in_=carry[:, h * P_:(h + 1) * P_])
    p.sig(nc.vector, ev)
    ev.wait(nc.tensor)
    return t[:, :]


_helpers_cache = globals().get("_helpers_cache", {})



def emit_rms_cols(p, xv, stats, scol, feat, eps):
    nc = p.nc
    c1 = stats[:, scol:scol + 1]
    nc.scalar.activation(out=p._lnsq[:, 0:feat], in_=xv, func=AF.Square,
                         accum_out=c1)
    p.barrier([nc.vector, nc.scalar])
    nc.vector.tensor_scalar(out=c1, in0=c1, scalar1=1.0 / feat, scalar2=None,
                            op0=ALU.mult)
    p.sig(nc.vector, p._msem)
    p._msem.wait(nc.scalar)
    nc.scalar.activation(out=c1, in_=c1, func=AF.Ln, bias=p.constcol(eps)[:, 0:1])
    p.chain(nc.scalar)
    nc.scalar.activation(out=c1, in_=c1, func=AF.Exp,
                         scale=p.constcol(-0.5)[:, 0:1])
    p.barrier([nc.vector, nc.scalar])
    return c1


def build_l2b1():
    p = Prog("l2b1")
    nc = p.nc
    FF = 4096
    x_in = p.inp("x", [TS, E])
    yg_in = p.inp("yg", [TS, 2048])
    woT = p.inp("woT", [2048, E])
    wgT = p.inp("wgT", [E, FF])
    wuT = p.inp("wuT", [E, FF])
    wmn = p.inp("wmn", [128, 16])
    wl2 = p.inp("wl2", [128, 8])
    idn_in = p.inp("idn", [128, 128], BF16)
    hid_out = p.outp("hid", [TS, E])
    hf_out = p.outp("hf", [TS, FF], BF16)

    wo_s = p.sb("wo_s", [128, 16 * E], BF16)
    wg_s = p.sb("wg_s", [128, 8 * FF], BF16)
    wu_s = p.sb("wu_s", [128, 8 * FF], BF16)
    wmn_s = p.sb("wmn_s", [128, 16])
    wl2_s = p.sb("wl2_s", [128, 8])
    idn = p.sb("idn_s", [128, 128], BF16)
    xv = p.sb("xv", [128, E])
    h0 = p.sb("h0", [128, E])
    yg = p.sb("yg_sb", [128, 2048])
    ygn = p.sb("ygn", [128, 2048], BF16)
    ygnT = p.sb("ygnT", [128, 16 * 128], BF16)
    hid = p.sb("hid_s1", [128, E])
    h2 = p.sb("h2", [128, E], BF16)
    h2T = p.sb("h2T", [128, 8 * 128], BF16)
    hfb = p.sb("hfb", [128, FF], BF16)
    sil = p.sb("sil", [128, 512])
    p._lnsq = ygn
    stats = p.sb("stats", [128, 16])
    ld = p.sem("ld"); cs = p.sem("cs"); st = p.sem("st"); mm = p.sem("mm")
    ev = MSem(p, "ev")
    p._msem = ev
    ps1 = p.ps("ps1", [128, 512])
    ps2 = p.ps("ps2", [128, 512])
    ps_g = p.ps("ps_g", [128, 512])
    ps_u = p.ps("ps_u", [128, 512])
    ptb = p.ps("ptb", [128, 128], BF16)
    p.constcol(1e-5); p.constcol(-0.5); p.constcol(1e-6)

    for kt in range(16):
        cs.n += 16
        nc.gpsimd.dma_start(out=wo_s[:, kt * E:(kt + 1) * E],
                            in_=dram_slice(woT, kt * 128, 128, 0, E)).then_inc(cs.h, 16)
    for kt in range(8):
        for (w, ws) in ((wgT, wg_s), (wuT, wu_s)):
            cs.n += 16
            nc.gpsimd.dma_start(out=ws[:, kt * FF:(kt + 1) * FF],
                                in_=dram_slice(w, kt * 128, 128, 0, FF)).then_inc(cs.h, 16)
    for (src, dst) in ((wmn, wmn_s), (wl2, wl2_s), (idn_in, idn)):
        ld.n += 16
        nc.sync.dma_start(out=dst[:, :], in_=src[:, :]).then_inc(ld.h, 16)
    for e in p.eng_all():
        e.wait_ge(ld.h, ld.n); e.wait_ge(cs.h, cs.n)
    p.barrier()
    for kt in range(16):
        nc.vector.tensor_scalar(out=wo_s[:, kt * E:(kt + 1) * E],
                                in0=wo_s[:, kt * E:(kt + 1) * E],
                                scalar1=wmn_s[:, kt:kt + 1], scalar2=None,
                                op0=ALU.mult)
    for kt in range(8):
        for ws in (wg_s, wu_s):
            nc.vector.tensor_scalar(out=ws[:, kt * FF:(kt + 1) * FF],
                                    in0=ws[:, kt * FF:(kt + 1) * FF],
                                    scalar1=wl2_s[:, kt:kt + 1], scalar2=None,
                                    op0=ALU.mult)
    p.barrier()
    for ti in range(16):
        r = slice(ti * 128, (ti + 1) * 128)
        ld.n += 16
        nc.sync.dma_start(out=xv[:, :], in_=x_in[r, :]).then_inc(ld.h, 16)
        ld.n += 16
        nc.sync.dma_start(out=yg[:, :], in_=yg_in[r, :]).then_inc(ld.h, 16)
        nc.vector.wait_ge(ld.h, ld.n)
        nc.scalar.wait_ge(ld.h, ld.n)
        ev.wait(nc.scalar)
        ev.wait(nc.vector)
        bcol, scol = emit_ln_stats(p, xv[:, :], stats, 0, E, 1e-5)
        nc.vector.tensor_scalar(out=h0[:, :], in0=xv[:, :], scalar1=scol,
                                scalar2=bcol, op0=ALU.mult, op1=ALU.add)
        p.sig(nc.vector, ev)
        ev.wait(nc.scalar)
        rc = emit_rms_cols(p, yg[:, :], stats, 8, 2048, 1e-5)
        nc.vector.tensor_scalar(out=ygn[:, :], in0=yg[:, :], scalar1=rc,
                                scalar2=None, op0=ALU.mult)
        p.sig(nc.vector, ev)
        ev.wait(nc.tensor)
        for it in range(16):
            nc.tensor.transpose(ptb[:, :], ygn[:, it * 128:(it + 1) * 128], idn[:, :])
            p.sig(nc.tensor, mm)
            nc.vector.wait_ge(mm.h, mm.n)
            nc.vector.tensor_copy(out=ygnT[:, it * 128:(it + 1) * 128], in_=ptb[:, :])
            p.sig(nc.vector, ev)
            ev.wait(nc.tensor)
        for oc in range(2):
            pso = ps1 if oc == 0 else ps2
            for it in range(16):
                nc.tensor.matmul(pso[:, :], ygnT[:, it * 128:(it + 1) * 128],
                                 wo_s[:, it * E + oc * 512: it * E + (oc + 1) * 512],
                                 start=(it == 0), stop=(it == 15))
        p.sig(nc.tensor, mm)
        nc.vector.wait_ge(mm.h, mm.n)
        nc.vector.tensor_add(out=hid[:, 0:512], in0=h0[:, 0:512], in1=ps1[:, :])
        nc.vector.tensor_add(out=hid[:, 512:1024], in0=h0[:, 512:1024], in1=ps2[:, :])
        p.sig(nc.vector, ev)
        ev.wait(nc.sync)
        ev.wait(nc.tensor)
        st.n += 16
        nc.sync.dma_start(out=hid_out[r, :], in_=hid[:, :]).then_inc(st.h, 16)
        ev.wait(nc.scalar)
        rc2 = emit_rms_cols(p, hid[:, :], stats, 9, E, 1e-6)
        nc.vector.tensor_scalar(out=h2[:, :], in0=hid[:, :], scalar1=rc2,
                                scalar2=None, op0=ALU.mult)
        p.sig(nc.vector, ev)
        ev.wait(nc.tensor)
        for it in range(8):
            nc.tensor.transpose(ptb[:, :], h2[:, it * 128:(it + 1) * 128], idn[:, :])
            p.sig(nc.tensor, mm)
            nc.vector.wait_ge(mm.h, mm.n)
            nc.vector.tensor_copy(out=h2T[:, it * 128:(it + 1) * 128], in_=ptb[:, :])
            p.sig(nc.vector, ev)
            ev.wait(nc.tensor)
        nc.scalar.wait_ge(st.h, st.n)
        for oc in range(8):
            o0 = oc * 512
            for kt in range(8):
                nc.tensor.matmul(ps_g[:, :], h2T[:, kt * 128:(kt + 1) * 128],
                                 wg_s[:, kt * FF + o0: kt * FF + o0 + 512],
                                 start=(kt == 0), stop=(kt == 7))
            for kt in range(8):
                nc.tensor.matmul(ps_u[:, :], h2T[:, kt * 128:(kt + 1) * 128],
                                 wu_s[:, kt * FF + o0: kt * FF + o0 + 512],
                                 start=(kt == 0), stop=(kt == 7))
            p.sig(nc.tensor, mm)
            nc.scalar.wait_ge(mm.h, mm.n)
            nc.scalar.activation(out=sil[:, :], in_=ps_g[:, :], func=AF.Silu)
            p.sig(nc.scalar, ev)
            ev.wait(nc.vector)
            nc.vector.tensor_tensor(out=hfb[:, o0:o0 + 512], in0=sil[:, :],
                                    in1=ps_u[:, :], op=ALU.mult)
            p.sig(nc.vector, ev)
            ev.wait(nc.tensor)
        ev.wait(nc.sync)
        st.n += 16
        nc.sync.dma_start(out=hf_out[r, :], in_=hfb[:, :]).then_inc(st.h, 16)
        nc.vector.wait_ge(st.h, st.n)
        nc.scalar.wait_ge(st.h, st.n)
    nc.sync.wait_ge(st.h, st.n)
    p.barrier()
    return p



def build_l2b2():
    p = Prog("l2b2")
    nc = p.nc
    FF = 4096
    hid_in = p.inp("hid", [TS, E])
    hf_in = p.inp("hf", [TS, FF], BF16)
    x_in = p.inp("x", [TS, E])
    wdT = p.inp("wdT", [FF, E])
    idn_in = p.inp("idn", [128, 128], BF16)
    x_out = p.outp("xo", [TS, E])
    wd_s = p.sb("wd_s", [128, 32 * E], BF16)
    idn = p.sb("idn_s", [128, 128], BF16)
    hf = p.sb("hf_sb", [128, FF], BF16)
    hfT = p.sb("hfT", [128, 32 * 128], BF16)
    hid = p.sb("hid_sb", [128, E])
    xb = p.sb("xb", [128, E])
    ob = p.sb("ob", [128, E])
    ld = p.sem("ld"); cs = p.sem("cs"); st = p.sem("st"); mm = p.sem("mm")
    ev = MSem(p, "ev")
    ps1 = p.ps("ps1", [128, 512])
    ps2 = p.ps("ps2", [128, 512])
    ptb = p.ps("ptb", [128, 128], BF16)
    for kt in range(32):
        cs.n += 16
        nc.gpsimd.dma_start(out=wd_s[:, kt * E:(kt + 1) * E],
                            in_=dram_slice(wdT, kt * 128, 128, 0, E)).then_inc(cs.h, 16)
    ld.n += 16
    nc.sync.dma_start(out=idn[:], in_=idn_in[:, :]).then_inc(ld.h, 16)
    for e in p.eng_all():
        e.wait_ge(ld.h, ld.n); e.wait_ge(cs.h, cs.n)
    p.barrier()
    for ti in range(16):
        r = slice(ti * 128, (ti + 1) * 128)
        nc.sync.wait_ge(st.h, st.n)
        ld.n += 16
        nc.sync.dma_start(out=hf[:, :], in_=hf_in[r, :]).then_inc(ld.h, 16)
        ld.n += 16
        nc.sync.dma_start(out=hid[:, :], in_=hid_in[r, :]).then_inc(ld.h, 16)
        ld.n += 16
        nc.sync.dma_start(out=xb[:, :], in_=x_in[r, :]).then_inc(ld.h, 16)
        nc.tensor.wait_ge(ld.h, ld.n)
        ev.wait(nc.tensor)
        for it in range(32):
            nc.tensor.transpose(ptb[:, :], hf[:, it * 128:(it + 1) * 128], idn[:, :])
            p.sig(nc.tensor, mm)
            nc.vector.wait_ge(mm.h, mm.n)
            nc.vector.tensor_copy(out=hfT[:, it * 128:(it + 1) * 128], in_=ptb[:, :])
            p.sig(nc.vector, ev)
            ev.wait(nc.tensor)
        for oc in range(2):
            pso = ps1 if oc == 0 else ps2
            for it in range(32):
                nc.tensor.matmul(pso[:, :], hfT[:, it * 128:(it + 1) * 128],
                                 wd_s[:, it * E + oc * 512: it * E + (oc + 1) * 512],
                                 start=(it == 0), stop=(it == 31))
        p.sig(nc.tensor, mm)
        nc.vector.wait_ge(mm.h, mm.n)
        nc.vector.wait_ge(st.h, st.n)
        nc.vector.wait_ge(ld.h, ld.n)
        nc.vector.tensor_add(out=ob[:, 0:512], in0=hid[:, 0:512], in1=ps1[:, :])
        nc.vector.tensor_add(out=ob[:, 512:1024], in0=hid[:, 512:1024], in1=ps2[:, :])
        p.chain(nc.vector)
        nc.vector.tensor_add(out=ob[:, :], in0=ob[:, :], in1=xb[:, :])
        p.sig(nc.vector, ev)
        ev.wait(nc.sync)
        ev.wait(nc.tensor)
        st.n += 16
        nc.sync.dma_start(out=x_out[r, :], in_=ob[:, :]).then_inc(st.h, 16)
    nc.sync.wait_ge(st.h, st.n)
    p.barrier()
    return p



def idnf_l2f(p):
    key = (id(p), "idnf_l2f")
    if key not in _helpers_cache:
        ap = p.inp("idnf", [128, 128], F32)
        sb_ = p.sb("idnf_s2", [128, 128], F32)
        _helpers_cache[key] = (ap, sb_)
    return _helpers_cache[key][1]


def build_l2ffn():
    p = Prog("l2ffn")
    nc = p.nc
    I2 = 2048
    x_in = p.inp("x", [TS, E])
    gwT = p.inp("gwT", [E, 8])
    w1T = [p.inp(f"w1T{e}", [E, I2]) for e in range(8)]
    w3T = [p.inp(f"w3T{e}", [E, I2]) for e in range(8)]
    w2T = [p.inp(f"w2T{e}", [I2, E]) for e in range(8)]
    idn_in = p.inp("idn", [128, 128], BF16)
    x_out = p.outp("xo", [TS, E])

    xnT = p.sb("xnT", [128, 16 * 8 * 128], BF16)
    acc = p.sb("acc", [128, 16 * E])
    w1_s = p.sb("w1_s", [128, 8 * I2], BF16)
    w3_s = p.sb("w3_s", [128, 8 * I2], BF16)
    w2_s = p.sb("w2_s", [128, 16 * E], BF16)
    gw_s = p.sb("gw_s", [128, 8 * 8])
    xnf = p.sb("xnf", [128, E])
    xnT32 = p.sb("xnT32", [128, 128])
    dw = p.sb("dw", [128, 16 * 8])
    lg = p.sb("lg", [128, 8 * 16])
    srt = p.sb("srt", [128, 8])
    xv = p.sb("xv", [128, E])
    xn = p.sb("xn", [128, E], BF16)
    p._lnsq = xn
    stats = p.sb("stats", [128, 16])
    actb = p.sb("actb", [128, 512], BF16)
    idn = p.sb("idn_s", [128, 128], BF16)
    ld = p.sem("ld"); cs = p.sem("cs"); st = p.sem("st"); mm = p.sem("mm")
    ev = MSem(p, "ev")
    ps1 = p.ps("ps1", [128, 512])
    ps2 = p.ps("ps2", [128, 512])
    ps3 = p.ps("ps3", [128, 512])
    ps4 = p.ps("ps4", [128, 512])
    ptb = p.ps("ptb", [128, 128], BF16)
    ptf = p.ps("ptf", [128, 128], F32)
    sil = xnf[:, 0:512]
    actT = xv[:, :].bitcast(BF16)
    p.constcol(1e-5); p.constcol(-0.5)

    ld.n += 16; nc.sync.dma_start(out=idn[:], in_=idn_in[:, :]).then_inc(ld.h, 16)
    _ifap, _ifsb = _helpers_cache[(id(p), "idnf_l2f")] if (id(p), "idnf_l2f") in _helpers_cache else (None, None)
    if _ifap is None:
        idnf_l2f(p)
        _ifap, _ifsb = _helpers_cache[(id(p), "idnf_l2f")]
    ld.n += 16; nc.sync.dma_start(out=_ifsb[:, :], in_=_ifap[:, :]).then_inc(ld.h, 16)
    for kt in range(8):
        ld.n += 16
        nc.sync.dma_start(out=gw_s[:, kt * 8:(kt + 1) * 8],
                          in_=dram_slice(gwT, kt * 128, 128, 0, 8)).then_inc(ld.h, 16)
    for e_ in p.eng_all():
        e_.wait_ge(ld.h, ld.n); e_.wait_ge(cs.h, cs.n)
    p.barrier()
    nc.vector.memset(acc[:, :], 0.0)
    for ti in range(16):
        r = slice(ti * 128, (ti + 1) * 128)
        ev.wait(nc.sync)
        ld.n += 16
        nc.sync.dma_start(out=xv[:, :], in_=x_in[r, :]).then_inc(ld.h, 16)
        nc.vector.wait_ge(ld.h, ld.n)
        nc.scalar.wait_ge(ld.h, ld.n)
        nc.scalar.wait_ge(mm.h, mm.n)
        ev.wait(nc.scalar)
        bcol, scol = emit_ln_stats(p, xv[:, :], stats, 0, E, 1e-5)
        nc.vector.wait_ge(mm.h, mm.n)
        nc.vector.tensor_scalar(out=xnf[:, :], in0=xv[:, :], scalar1=scol,
                                scalar2=bcol, op0=ALU.mult, op1=ALU.add)
        p.chain(nc.vector)
        nc.vector.tensor_copy(out=xn[:, :], in_=xnf[:, :])
        p.sig(nc.vector, ev)
        ev.wait(nc.tensor)
        for kt in range(8):
            nc.tensor.transpose(ptb[:, :], xn[:, kt * 128:(kt + 1) * 128], idn[:, :])
            nc.tensor.transpose(ptf[:, :], xnf[:, kt * 128:(kt + 1) * 128],
                                idnf_l2f(p)[:, :])
            p.sig(nc.tensor, mm)
            nc.vector.wait_ge(mm.h, mm.n)
            nc.vector.tensor_copy(
                out=xnT[:, (ti * 8 + kt) * 128:(ti * 8 + kt + 1) * 128],
                in_=ptb[:, :])
            nc.vector.tensor_copy(out=xnT32[:, :], in_=ptf[:, :])
            p.sig(nc.vector, ev)
            ev.wait(nc.tensor)
            nc.tensor.matmul(ps3[:, 0:8], xnT32[:, :],
                             gw_s[:, kt * 8:(kt + 1) * 8],
                             start=(kt == 0), stop=(kt == 7))
            p.sig(nc.tensor, mm)
            nc.vector.wait_ge(mm.h, mm.n)
        p.sig(nc.tensor, mm)
        nc.vector.wait_ge(mm.h, mm.n)
        s8 = lg[:, 0:8]; ab = lg[:, 8:16]; f1 = lg[:, 16:24]; d1 = lg[:, 24:32]
        kp = lg[:, 32:40]; e1 = lg[:, 40:48]; tmp = lg[:, 48:56]; ne1 = lg[:, 56:64]
        e2 = lg[:, 64:72]; f2 = lg[:, 72:80]; d2 = lg[:, 80:88]
        nc.vector.tensor_copy(out=s8, in_=ps3[:, 0:8])
        p.chain(nc.vector)
        nc.vector.max(out=srt[:, 0:8], in_=s8)
        p.sig(nc.vector, ev)
        ev.wait(nc.scalar)
        nc.scalar.activation(out=ab, in_=s8, func=AF.Abs)
        p.sig(nc.scalar, ev)
        ev.wait(nc.vector)
        for (tcol, fv, dv) in ((srt[:, 0:1], f1, d1), (srt[:, 1:2], f2, d2)):
            nc.vector.tensor_scalar(out=fv, in0=ab, scalar1=tcol, scalar2=None,
                                    op0=ALU.max)
            nc.vector.tensor_scalar(out=dv, in0=s8, scalar1=tcol, scalar2=None,
                                    op0=ALU.subtract)
        p.chain(nc.vector)
        for (fv, dv, kpv) in ((f1, d1, kp), (f2, d2, tmp)):
            nc.vector.scalar_tensor_tensor(out=fv, in0=fv, scalar=0.02, in1=dv,
                                           op0=ALU.mult, op1=ALU.add)
            p.chain(nc.vector)
            nc.vector.tensor_single_scalar(out=kpv, in_=fv, scalar=0.0, op=ALU.is_ge)
            p.chain(nc.vector)
        nc.vector.tensor_scalar(out=ne1, in0=s8, scalar1=srt[:, 0:1], scalar2=None,
                                op0=ALU.is_equal)
        p.chain(nc.vector)
        p.sig(nc.vector, ev)
        ev.wait(nc.scalar)
        nc.scalar.activation(out=d1, in_=d1, func=AF.Exp)
        p.chain(nc.scalar)
        nc.scalar.activation(out=d2, in_=d2, func=AF.Exp)
        p.sig(nc.scalar, ev)
        ev.wait(nc.vector)
        nc.vector.tensor_mul(out=e1, in0=d1, in1=kp)
        nc.vector.tensor_mul(out=e2, in0=d2, in1=tmp)
        p.chain(nc.vector)
        nc.vector.scalar_tensor_tensor(out=e2, in0=ne1, scalar=-1.0, in1=e2,
                                       op0=ALU.mult, op1=ALU.bypass) if False else None
        nc.vector.tensor_scalar(out=tmp, in0=ne1, scalar1=-1.0, scalar2=1.0,
                                op0=ALU.mult, op1=ALU.add)
        p.chain(nc.vector)
        nc.vector.tensor_mul(out=e2, in0=e2, in1=tmp)
        p.chain(nc.vector)
        nc.vector.reduce_sum(f1[:, 0:1], e1, axis=AX.X)
        nc.vector.reduce_sum(f2[:, 0:1], e2, axis=AX.X)
        p.chain(nc.vector)
        nc.vector.reciprocal(out=f1[:, 0:1], in_=f1[:, 0:1])
        p.chain(nc.vector)
        nc.vector.reciprocal(out=f2[:, 0:1], in_=f2[:, 0:1])
        p.chain(nc.vector)
        nc.vector.tensor_scalar(out=d1, in0=s8, scalar1=srt[:, 1:2], scalar2=None,
                                op0=ALU.is_equal)
        p.chain(nc.vector)
        nc.vector.tensor_mul(out=d1, in0=d1, in1=tmp)
        p.chain(nc.vector)
        nc.vector.tensor_scalar(out=ne1, in0=ne1, scalar1=f1[:, 0:1], scalar2=None,
                                op0=ALU.mult)
        nc.vector.tensor_scalar(out=d1, in0=d1, scalar1=f2[:, 0:1], scalar2=None,
                                op0=ALU.mult)
        p.chain(nc.vector)
        nc.vector.tensor_add(out=dw[:, ti * 8:(ti + 1) * 8], in0=ne1, in1=d1)
        p.sig(nc.vector, ev)
    p.barrier()
    for e in range(8):
        nc.gpsimd.wait_ge(mm.h, mm.n)
        for kt in range(8):
            cs.n += 16
            nc.gpsimd.dma_start(out=w1_s[:, kt * I2:(kt + 1) * I2],
                                in_=dram_slice(w1T[e], kt * 128, 128, 0, I2)
                                ).then_inc(cs.h, 16)
            cs.n += 16
            nc.gpsimd.dma_start(out=w3_s[:, kt * I2:(kt + 1) * I2],
                                in_=dram_slice(w3T[e], kt * 128, 128, 0, I2)
                                ).then_inc(cs.h, 16)
        for it in range(16):
            cs.n += 16
            nc.gpsimd.dma_start(out=w2_s[:, it * E:(it + 1) * E],
                                in_=dram_slice(w2T[e], it * 128, 128, 0, E)
                                ).then_inc(cs.h, 16)
        for e_ in (nc.tensor, nc.vector, nc.scalar):
            e_.wait_ge(cs.h, cs.n)
        for ti in range(16):
            xb = ti * 8 * 128
            for oc in range(4):
                o0 = oc * 512
                for kt in range(8):
                    nc.tensor.matmul(ps1[:, :], xnT[:, xb + kt * 128: xb + (kt + 1) * 128],
                                     w1_s[:, kt * I2 + o0: kt * I2 + o0 + 512],
                                     start=(kt == 0), stop=(kt == 7))
                for kt in range(8):
                    nc.tensor.matmul(ps2[:, :], xnT[:, xb + kt * 128: xb + (kt + 1) * 128],
                                     w3_s[:, kt * I2 + o0: kt * I2 + o0 + 512],
                                     start=(kt == 0), stop=(kt == 7))
                p.sig(nc.tensor, mm)
                nc.scalar.wait_ge(mm.h, mm.n)
                nc.scalar.activation(out=sil, in_=ps1[:, :], func=AF.Silu)
                p.sig(nc.scalar, ev)
                ev.wait(nc.vector)
                nc.vector.tensor_tensor(out=actb[:, :], in0=sil, in1=ps2[:, :],
                                        op=ALU.mult)
                p.sig(nc.vector, ev)
                ev.wait(nc.tensor)
                for j in range(4):
                    nc.tensor.transpose(ptb[:, :], actb[:, j * 128:(j + 1) * 128],
                                        idn[:, :])
                    p.sig(nc.tensor, mm)
                    nc.vector.wait_ge(mm.h, mm.n)
                    nc.vector.tensor_copy(
                        out=actT[:, (oc * 4 + j) * 128:(oc * 4 + j + 1) * 128],
                        in_=ptb[:, :])
                    p.sig(nc.vector, ev)
                    ev.wait(nc.tensor)
            for oc2 in range(2):
                pso = ps3 if oc2 == 0 else ps4
                for it in range(16):
                    nc.tensor.matmul(pso[:, :], actT[:, it * 128:(it + 1) * 128],
                                     w2_s[:, it * E + oc2 * 512: it * E + (oc2 + 1) * 512],
                                     start=(it == 0), stop=(it == 15))
            p.sig(nc.tensor, mm)
            nc.vector.wait_ge(mm.h, mm.n)
            dcol = dw[:, ti * 8 + e: ti * 8 + e + 1]
            nc.vector.scalar_tensor_tensor(out=acc[:, ti * E: ti * E + 512],
                                           in0=ps3[:, :], scalar=dcol,
                                           in1=acc[:, ti * E: ti * E + 512],
                                           op0=ALU.mult, op1=ALU.add)
            nc.vector.scalar_tensor_tensor(out=acc[:, ti * E + 512:(ti + 1) * E],
                                           in0=ps4[:, :], scalar=dcol,
                                           in1=acc[:, ti * E + 512:(ti + 1) * E],
                                           op0=ALU.mult, op1=ALU.add)
            p.sig(nc.vector, ev)
            ev.wait(nc.tensor)
    p.barrier()
    for ti in range(16):
        r = slice(ti * 128, (ti + 1) * 128)
        nc.sync.wait_ge(st.h, st.n)
        ld.n += 16
        nc.sync.dma_start(out=xv[:, :], in_=x_in[r, :]).then_inc(ld.h, 16)
        nc.vector.wait_ge(ld.h, ld.n)
        nc.vector.wait_ge(st.h, st.n)
        nc.vector.tensor_add(out=xv[:, :], in0=xv[:, :], in1=acc[:, ti * E:(ti + 1) * E])
        p.sig(nc.vector, ev)
        ev.wait(nc.sync)
        st.n += 16
        nc.sync.dma_start(out=x_out[r, :], in_=xv[:, :]).then_inc(st.h, 16)
    nc.sync.wait_ge(st.h, st.n)
    p.barrier()
    return p



def build_l3a():
    p = Prog("l3a")
    nc = p.nc
    HH = 4
    Dh = 64
    scale = Dh ** -0.5
    x = p.inp("x", [S, E])
    Wq = p.inp("Wq", [E, 512]); Wk = p.inp("Wk", [E, 512]); Wv = p.inp("Wv", [E, 512])
    Wo = p.inp("Wo", [512, E])
    idn_in = p.inp("idn", [128, 128], BF16)
    idnf_in = p.inp("idnf", [128, 128], F32)
    bdm_in = p.inp("bdm", [128, 128], BF16)
    part = p.outp("part", [S, E])

    Qaug = p.sb("Qaug", [128, HH * S], BF16)
    Kaug = p.sb("Kaug", [128, HH * S], BF16)
    Vb = p.sb("Vb", [128, 32 * HH * 65], BF16)
    ao_tok = p.sb("ao_tok", [128, 32 * 4 * 64], BF16)
    aoT = p.sb("aoT", [128, 4 * S], BF16)
    xnT = p.sb("xnT", [128, 8 * 512], BF16)
    x_tok = p.sb("x_tok", [128, E])
    xn_tok = p.sb("xn_tok", [128, E], BF16)
    p._lnsq = xn_tok
    stats = p.sb("stats", [128, 16])
    idn = p.sb("idn_s", [128, 128], BF16)
    idnf = p.sb("idnf_s", [128, 128], F32)
    bdm = p.sb("bdm_s", [128, 128], BF16)
    ones64f = p.sb("ones64f", [64, 1], F32)
    ones1f = p.sb("ones1f", [1, 128], F32)
    wq_s = p.sb("wq_s", [128, 8 * 512], BF16)
    wk_s = p.sb("wk_s", [128, 8 * 512], BF16)
    wv_s = p.sb("wv_s", [128, 8 * 512], BF16)
    wo_s = p.sb("wo_s", [128, 4 * E], BF16)
    vt_scr = p.sb("vt_scr", [128, 512], BF16)
    qsq = p.sb("qsq", [64, 512])
    rr_tmp = p.sb("rr_tmp", [1, 512])
    crow = rr_tmp
    tr1 = p.sb("tr1", [1, 1])
    kn2 = p.sb("kn2", [1, 16])
    PTl = p.sb("PTl", [128, 128], BF16)
    cvT = p.sb("cvT", [128, 2 * 64])
    cvs = p.sb("cvs", [128, 2 * 64])
    ckb = p.sb("ckb", [128, 4 * 64])
    cks = p.sb("cks", [128, 4 * 64])
    ckq = p.sb("ckq", [128, 4 * 64], BF16)
    onesq = p.sb("onesq", [128, 64])
    ones11 = p.sb("ones11", [1, 1])
    ntile = p.sb("ntile", [128, 64])
    pairb = p.sb("pairb", [128, 128], BF16)
    outb = p.sb("outb", [128, E])
    cvtok = p.sb("cvtok", [64, 4 * 64])
    cumvx = p.sb("cumvx", [128, 2048], BF16)
    ccol = p.sb("ccol", [128, 32])
    rcol = p.sb("rcol", [128, 1])
    cvscr = p.dram("cvscr", [8 * 64 * 64])
    p._l3cv = lambda f: cvT[:, f * 64:(f + 1) * 64]
    p._l3cvs = lambda f: cvs[:, f * 64:(f + 1) * 64]
    p._l3onesf = lambda f: onesq
    p._l3ck = lambda h: ckb[0:64, h * 64:(h + 1) * 64] if False else ckb.__getitem__((slice(0, 64), slice(h * 64, (h + 1) * 64)))
    p._l3cks = lambda h: cks.__getitem__((slice(0, 64), slice(h * 64, (h + 1) * 64)))
    p._l3ones = lambda h: onesq.__getitem__((slice(0, 64), slice(0, 64)))
    p._ones11 = lambda p_: ones11.__getitem__((slice(0, 1), slice(0, 1)))
    p._l3nt = lambda p_: ntile
    p._l3pair = lambda p_: pairb
    p._l3ob = lambda p_: outb
    ld = p.sem("ld"); cs = p.sem("cs"); st = p.sem("st"); mm = p.sem("mm")
    ev = MSem(p, "ev")
    ps1 = p.ps("ps1", [128, 512])
    ps2 = p.ps("ps2", [128, 512])
    psAV = p.ps("psAV", [128, 512])
    psC = p.ps("psC", [128, 512])
    ptb = p.ps("ptb", [128, 128], BF16)
    ptf = p.ps("ptf", [128, 128], F32)
    p.constcol(1e-5); p.constcol(-0.5)

    for (srcb, dstb) in ((idn_in, idn), (idnf_in, idnf), (bdm_in, bdm)):
        ld.n += 16; nc.sync.dma_start(out=dstb[:, :], in_=srcb[:, :]).then_inc(ld.h, 16)
    for kt in range(8):
        for (w, ws) in ((Wq, wq_s), (Wk, wk_s), (Wv, wv_s)):
            cs.n += 16
            nc.gpsimd.dma_start(out=ws[:, kt * 512:(kt + 1) * 512],
                                in_=dram_slice(w, kt * 128, 128, 0, 512)).then_inc(cs.h, 16)
    for kt in range(4):
        cs.n += 16
        nc.gpsimd.dma_start(out=wo_s[:, kt * E:(kt + 1) * E],
                            in_=dram_slice(Wo, kt * 128, 128, 0, E)).then_inc(cs.h, 16)
    for e in p.eng_all():
        e.wait_ge(ld.h, ld.n); e.wait_ge(cs.h, cs.n)
    p.barrier()
    nc.vector.memset(ones64f[:, :], 1.0)
    nc.vector.memset(ones1f[:, :], 1.0)
    nc.vector.memset(kn2[:, :], 0.0)
    nc.vector.memset(onesq[:, :], 1.0)
    nc.vector.memset(ones11[:, :], 1.0)
    p.barrier()

    for hp in range(2):
        nc.vector.memset(cvT[:, :], 0.0)
        p.barrier()
        for g in range(8):
            g0 = g * 512
            for tt in range(4):
                t0 = g0 + tt * 128
                xv = x_tok[:, :]
                p.barrier([nc.sync, nc.vector, nc.scalar])
                ld.n += 16
                nc.sync.dma_start(out=xv, in_=x[t0:t0 + 128, :]).then_inc(ld.h, 16)
                nc.vector.wait_ge(ld.h, ld.n)
                nc.scalar.wait_ge(ld.h, ld.n)
                nc.scalar.wait_ge(mm.h, mm.n)
                bcol, scol = emit_ln_stats(p, xv, stats, 0, E, 1e-5)
                nc.vector.wait_ge(mm.h, mm.n)
                nc.vector.tensor_scalar(out=xn_tok[:, :], in0=xv, scalar1=scol,
                                        scalar2=bcol, op0=ALU.mult, op1=ALU.add)
                p.sig(nc.vector, ev)
                ev.wait(nc.tensor)
                for kt in range(8):
                    nc.tensor.transpose(ptb[:, :], xn_tok[:, kt * 128:(kt + 1) * 128],
                                        idn[:, :])
                    p.sig(nc.tensor, mm)
                    nc.vector.wait_ge(mm.h, mm.n)
                    nc.vector.tensor_copy(
                        out=xnT[:, kt * 512 + tt * 128: kt * 512 + (tt + 1) * 128],
                        in_=ptb[:, :])
                    p.sig(nc.vector, ev)
                    ev.wait(nc.tensor)
            p.barrier()
            for (ws, dst, sc) in ((wq_s, Qaug, scale), (wk_s, Kaug, 1.0)):
                for f in range(2):
                    c0 = hp * 256 + f * 128
                    for kt in range(8):
                        nc.tensor.matmul(ps1[:, :],
                                         ws[:, kt * 512 + c0: kt * 512 + c0 + 128],
                                         xnT[:, kt * 512:(kt + 1) * 512],
                                         start=(kt == 0), stop=(kt == 7))
                    p.sig(nc.tensor, mm)
                    nc.vector.wait_ge(mm.h, mm.n)
                    for hh2 in range(2):
                        h = f * 2 + hh2
                        nc.vector.tensor_scalar(
                            out=dst[0:64, h * S + g0: h * S + g0 + 512],
                            in0=ps1[hh2 * 64:(hh2 + 1) * 64, :], scalar1=float(sc),
                            scalar2=None, op0=ALU.mult)
                    p.sig(nc.vector, ev)
                    ev.wait(nc.tensor)
            for f in range(2):
                c0 = hp * 256 + f * 128
                for kt in range(8):
                    nc.tensor.matmul(ps2[:, :],
                                     wv_s[:, kt * 512 + c0: kt * 512 + c0 + 128],
                                     xnT[:, kt * 512:(kt + 1) * 512],
                                     start=(kt == 0), stop=(kt == 7))
                p.sig(nc.tensor, mm)
                nc.vector.wait_ge(mm.h, mm.n)
                nc.vector.tensor_copy(out=vt_scr[:, :], in_=ps2[:, :])
                p.chain(nc.vector)
                nc.vector.tensor_reduce(
                    out=p._l3cv(f)[:, g * 8:(g + 1) * 8],
                    in_=vt_scr[:, :].rearrange("p (c k) -> p c k", k=64),
                    axis=AX.X, op=ALU.add)
                p.sig(nc.vector, ev)
                ev.wait(nc.tensor)
                for tt in range(4):
                    ktg = g * 4 + tt
                    nc.tensor.transpose(ptb[:, :], vt_scr[:, tt * 128:(tt + 1) * 128],
                                        idn[:, :])
                    p.sig(nc.tensor, mm)
                    nc.vector.wait_ge(mm.h, mm.n)
                    for hh2 in range(2):
                        h = f * 2 + hh2
                        nc.vector.tensor_copy(
                            out=Vb[:, ktg * HH * 65 + h * 65: ktg * HH * 65 + h * 65 + 64],
                            in_=ptb[:, hh2 * 64:(hh2 + 1) * 64])
                    p.sig(nc.vector, ev)
                    ev.wait(nc.tensor)
            for h in range(HH):
                nc.scalar.wait_ge(mm.h, mm.n)
                nc.scalar.activation(out=qsq[0:64, :],
                                     in_=Kaug[0:64, h * S + g0: h * S + g0 + 512],
                                     func=AF.Square)
                p.sig(nc.scalar, ev)
                ev.wait(nc.tensor)
                nc.tensor.matmul(ps2[0:1, :], ones64f[0:64, :], qsq[0:64, :],
                                 start=True, stop=True)
                p.sig(nc.tensor, mm)
                nc.vector.wait_ge(mm.h, mm.n)
                nc.vector.reduce_max(tr1[0:1, 0:1], ps2[0:1, 0:512], axis=AX.X)
                p.chain(nc.vector)
                hcol = kn2[0:1, hp * HH + h: hp * HH + h + 1]
                if g == 0:
                    nc.vector.tensor_copy(out=hcol, in_=tr1[0:1, 0:1])
                else:
                    nc.vector.tensor_max(out=hcol, in0=hcol, in1=tr1[0:1, 0:1])
                p.sig(nc.vector, ev)
                ev.wait(nc.tensor)
        p.barrier()
        nc.vector.memset(Kaug[64:65, :], 1.0)
        for kt in range(32):
            va = AP(tensor=Vb.tensor, offset=Vb.offset + kt * HH * 65 + 64,
                    ap=[[Vb.shape[1], 128], [65, HH], [1, 1]])
            nc.vector.memset(va, 1.0)
        p.barrier()
        for h in range(HH):
            for g in range(8):
                cpos = slice(h * S + g * 512, h * S + (g + 1) * 512)
                nc.scalar.wait_ge(mm.h, mm.n)
                nc.scalar.activation(out=qsq[0:64, :], in_=Qaug[0:64, cpos],
                                     func=AF.Square)
                p.sig(nc.scalar, ev)
                ev.wait(nc.tensor)
                nc.tensor.matmul(ps2[0:1, :], ones64f[0:64, :], qsq[0:64, :],
                                 start=True, stop=True)
                p.sig(nc.tensor, mm)
                nc.scalar.wait_ge(mm.h, mm.n)
                nc.scalar.activation(out=crow[0:1, :], in_=ps2[0:1, 0:512],
                                     func=AF.Sqrt,
                                     scale=kn2[0:1, hp * HH + h: hp * HH + h + 1])
                p.sig(nc.scalar, ev)
                ev.wait(nc.vector)
                nc.vector.tensor_scalar(out=Qaug[64:65, cpos], in0=crow[0:1, :],
                                        scalar1=-1.0, scalar2=None, op0=ALU.mult)
                p.sig(nc.vector, ev)
                ev.wait(nc.tensor)
                nc.scalar.wait_ge(ev.sub(nc.vector).h, ev.sub(nc.vector).n)
        p.barrier()
        for h in range(HH):
            hb = h * S
            nc.vector.tensor_reduce(
                out=p._l3ck(h)[0:64, :],
                in_=Kaug[0:64, hb:hb + S].rearrange("p (c k) -> p c k", k=64),
                axis=AX.X, op=ALU.add)
            p.chain(nc.vector)
        for h in range(HH):
            nc.vector.tensor_tensor_scan(out=p._l3cks(h),
                                         data0=p._l3ones(h),
                                         data1=p._l3ck(h), initial=0.0,
                                         op0=ALU.mult, op1=ALU.add)
            p.chain(nc.vector)
        p.barrier([nc.vector, nc.scalar])
        for h in range(HH):
            nc.vector.memset(ckq[0:64, h * 64: h * 64 + 1], 0.0)
            nc.vector.tensor_scalar(out=ckq[0:64, h * 64 + 1: h * 64 + 64],
                                    in0=p._l3cks(h)[0:64, 0:63],
                                    scalar1=1.0 / 64.0, scalar2=None, op0=ALU.mult)
            p.sig(nc.vector, ev)
        p.barrier()
        for f in range(2):
            cv = p._l3cv(f)
            nc.vector.tensor_tensor_scan(out=p._l3cvs(f)[:, :],
                                         data0=p._l3onesf(f)[:, :],
                                         data1=cv[:, :], initial=0.0,
                                         op0=ALU.mult, op1=ALU.add)
            p.chain(nc.vector)
            nc.vector.memset(cv[:, 0:1], 0.0)
            nc.vector.tensor_scalar(out=cv[:, 1:64], in0=p._l3cvs(f)[:, 0:63],
                                    scalar1=0.5 / 64.0, scalar2=None, op0=ALU.mult)
            p.sig(nc.vector, ev)
        p.barrier()
        for h in range(HH):
            f, r0 = h // 2, (h % 2) * 64
            nc.tensor.transpose(ptf[0:64, 0:64],
                                p._l3cv(f)[r0:r0 + 64, :],
                                idnf[r0:r0 + 64, r0:r0 + 64])
            p.sig(nc.tensor, mm)
            nc.vector.wait_ge(mm.h, mm.n)
            nc.vector.tensor_copy(out=cvtok[0:64, h * 64:(h + 1) * 64],
                                  in_=ptf[0:64, 0:64])
            p.sig(nc.vector, ev)
            ev.wait(nc.tensor)
        ev.wait(nc.sync)
        st.n += 16
        nc.sync.dma_start(out=cvscr[hp * 4 * 64 * 64:(hp + 1) * 4 * 64 * 64],
                          in_=cvtok[0:64, 0:256]).then_inc(st.h, 16)
        nc.sync.wait_ge(st.h, st.n)
        p.barrier()
        for h in range(HH):
            hg = hp * HH + h
            hb = h * S
            ev.wait(nc.gpsimd)
            for po in range(2):
                srcx = AP(tensor=cvscr.tensor,
                          offset=hp * 16384 + po * 256 + h * 64,
                          ap=[[0, 64], [512, 32], [1, 64]])
                cs.n += 16
                nc.gpsimd.dma_start(
                    out=cumvx[po * 64:(po + 1) * 64, :].rearrange("p (t d) -> p t d", d=64),
                    in_=srcx).then_inc(cs.h, 16)
            nc.tensor.wait_ge(cs.h, cs.n)
            nc.vector.wait_ge(cs.h, cs.n)
            for gq in range(8):
                for cc in range(8):
                    cidx = gq * 8 + cc
                    nc.tensor.matmul(psC[0:1, cc * 64:(cc + 1) * 64],
                                     ckq[0:64, h * 64 + cidx: h * 64 + cidx + 1],
                                     Qaug[0:64, hb + cidx * 64: hb + (cidx + 1) * 64],
                                     start=True, stop=True)
                p.sig(nc.tensor, mm)
                nc.scalar.wait_ge(mm.h, mm.n)
                nc.scalar.activation(out=rr_tmp[0:1, :], in_=psC[0:1, 0:512],
                                     func=AF.Sigmoid)
                p.sig(nc.scalar, ev)
                ev.wait(nc.tensor)
                for j in range(4):
                    nc.tensor.matmul(psC[:, 448 + j:449 + j],
                                     rr_tmp[0:1, j * 128:(j + 1) * 128],
                                     ones11[0:1, 0:1], start=True, stop=True)
                p.sig(nc.tensor, mm)
                nc.vector.wait_ge(mm.h, mm.n)
                nc.vector.tensor_copy(out=ccol[:, gq * 4:(gq + 1) * 4],
                                      in_=psC[:, 448:452])
                p.sig(nc.vector, ev)
                ev.wait(nc.tensor)
            for t in range(32):
                nc.tensor.matmul(ps1[:, 0:128],
                                 Kaug[0:65, hb + t * 128: hb + (t + 1) * 128],
                                 Qaug[0:65, hb + t * 128: hb + (t + 1) * 128],
                                 start=True, stop=False)
                nc.tensor.matmul(ps1[:, 0:128], idn[:, :], bdm[:, :],
                                 start=False, stop=True)
                p.sig(nc.tensor, mm)
                nc.scalar.wait_ge(mm.h, mm.n)
                nc.scalar.activation(out=PTl[:, :], in_=ps1[:, 0:128], func=AF.Exp)
                p.sig(nc.scalar, ev)
                ev.wait(nc.tensor)
                nc.tensor.matmul(psAV[:, 0:65], PTl[:, :],
                                 Vb[:, t * HH * 65 + h * 65: t * HH * 65 + (h + 1) * 65],
                                 start=True, stop=True)
                p.sig(nc.tensor, mm)
                nc.vector.wait_ge(mm.h, mm.n)
                nc.vector.reciprocal(out=rcol[:, :], in_=psAV[:, 64:65])
                p.chain(nc.vector)
                nc.vector.tensor_scalar(out=ntile[:, :], in0=psAV[:, 0:64],
                                        scalar1=rcol[:, 0:1], scalar2=None,
                                        op0=ALU.mult)
                p.chain(nc.vector)
                nc.vector.scalar_tensor_tensor(
                    out=ao_tok[:, (t * HH + h) * 64: (t * HH + h) * 64 + 64],
                    in0=cumvx[:, t * 64:(t + 1) * 64],
                    scalar=ccol[:, t:t + 1],
                    in1=ntile[:, :], op0=ALU.mult, op1=ALU.add)
                p.sig(nc.vector, ev)
                ev.wait(nc.tensor)
        p.barrier()
        for t in range(32):
            for f in range(2):
                h0_, h1_ = 2 * f, 2 * f + 1
                nc.vector.wait_ge(mm.h, mm.n)
                nc.vector.tensor_copy(
                    out=pairb[:, 0:64],
                    in_=ao_tok[:, (t * HH + h0_) * 64: (t * HH + h0_) * 64 + 64])
                nc.vector.tensor_copy(
                    out=pairb[:, 64:128],
                    in_=ao_tok[:, (t * HH + h1_) * 64: (t * HH + h1_) * 64 + 64])
                p.sig(nc.vector, ev)
                ev.wait(nc.tensor)
                nc.tensor.transpose(ptb[:, :], pairb[:, :], idn[:, :])
                p.sig(nc.tensor, mm)
                nc.vector.wait_ge(mm.h, mm.n)
                pt_ = hp * 2 + f
                nc.vector.tensor_copy(
                    out=aoT[:, pt_ * S + t * 128: pt_ * S + (t + 1) * 128],
                    in_=ptb[:, :])
                p.sig(nc.vector, ev)
                ev.wait(nc.tensor)
        p.barrier()
    for tt in range(32):
        t0 = tt * 128
        ev.wait(nc.tensor)
        for oc in range(2):
            pso = ps1 if oc == 0 else ps2
            for f in range(4):
                nc.tensor.matmul(pso[:, :], aoT[:, f * S + t0: f * S + t0 + 128],
                                 wo_s[:, f * E + oc * 512: f * E + (oc + 1) * 512],
                                 start=(f == 0), stop=(f == 3))
        p.sig(nc.tensor, mm)
        nc.vector.wait_ge(mm.h, mm.n)
        nc.vector.wait_ge(st.h, st.n)
        nc.vector.tensor_copy(out=outb[:, 0:512], in_=ps1[:, :])
        nc.vector.tensor_copy(out=outb[:, 512:1024], in_=ps2[:, :])
        p.sig(nc.vector, ev)
        ev.wait(nc.sync)
        st.n += 16
        nc.sync.dma_start(out=part[t0:t0 + 128, :], in_=outb[:, :]).then_inc(st.h, 16)
    nc.sync.wait_ge(st.h, st.n)
    p.barrier()
    return p



def build_l3fa():
    p = Prog("l3fa")
    nc = p.nc
    NAUX = 4096
    x_in = p.inp("x", [TS, E])
    pA = p.inp("pA", [TS, E]); pB = p.inp("pB", [TS, E])
    qpT = p.inp("qpT", [E, 128])
    rwT = p.inp("rwT", [128, 48])
    akT = p.inp("akT", [128, NAUX])
    avals = p.inp("avals", [NAUX, E])
    rel = p.inp("rel", [1, NAUX])
    uw1T = p.inp("uw1T", [E, 256])
    ub1 = p.inp("ub1", [128, 2])
    uw2T = p.inp("uw2T", [256, 1])
    idn_in = p.inp("idn", [128, 128], BF16)
    idnf_in = p.inp("idnf", [128, 128], F32)
    x7_out = p.outp("x7", [TS, E])
    agg_out = p.outp("agg", [TS, E])
    lrn_out = p.outp("lrn", [TS, 1])
    lvp_out = p.outp("lvp", [1, 1])

    qp_s = p.sb("qp_s", [128, 8 * 128])
    rw_s = p.sb("rw_s", [128, 48])
    ak_s = p.sb("ak_s", [128, NAUX])
    av_s = p.sb("av_s", [128, 32 * E], BF16)
    rkT = p.sb("rkT", [128, NAUX])
    uw1_s = p.sb("uw1_s", [128, 8 * 256], BF16)
    ub1_s = p.sb("ub1_s", [128, 2])
    uw2_s = p.sb("uw2_s", [128, 2], BF16)
    idn = p.sb("idn_s", [128, 128], BF16)
    idnf = p.sb("idnf_s", [128, 128], F32)
    xv = p.sb("xv", [128, E])
    w1s = p.sb("w1s", [128, E]); w2s = p.sb("w2s", [128, E])
    hn = p.sb("hn", [128, E])
    hnb = p.sb("hnb", [128, E], BF16)
    hnbT = p.sb("hnbT", [128, 128], BF16)
    hnT32 = p.sb("hnT32", [128, 128])
    qrow = p.sb("qrow", [128, 128])
    qT = p.sb("qT", [128, 128])
    rq = p.sb("rq", [128, 48])
    rqpad = p.sb("rqpad", [128, 128])
    rqT = p.sb("rqT", [128, 128])
    srow = p.sb("srow", [128, NAUX])
    m01 = p.sb("m01", [128, NAUX], BF16)
    top8 = p.sb("top8", [128, 8])
    wq8 = p.sb("wq8", [128, NAUX], BF16)
    wqT = p.sb("wqT", [128, 32 * 128], BF16)
    h1b = p.sb("h1b", [128, 256], BF16)
    h1T = p.sb("h1T", [128, 2 * 128], BF16)
    stats = p.sb("stats", [128, 16])
    lvc = p.sb("lvc", [128, 16])
    ob = p.sb("ob", [128, E])
    p._lnsq = ob
    ld = p.sem("ld"); cs = p.sem("cs"); st = p.sem("st"); mm = p.sem("mm")
    ev = MSem(p, "ev")
    ps1 = p.ps("ps1", [128, 512])
    ps2 = p.ps("ps2", [128, 512])
    ps3 = p.ps("ps3", [128, 512])
    ptb = p.ps("ptb", [128, 128], BF16)
    ptf = p.ps("ptf", [128, 128], F32)
    p.constcol(1e-5); p.constcol(-0.5); cone = p.constcol(1.0)

    for (srcb, dstb, nr) in ((idn_in, idn, 128), (idnf_in, idnf, 128),
                             (rwT, rw_s, 128), (ub1, ub1_s, 128)):
        ld.n += 16
        nc.sync.dma_start(out=dstb[0:nr, :], in_=srcb[:, :]).then_inc(ld.h, 16)
    for kt in range(8):
        ld.n += 16
        nc.sync.dma_start(out=qp_s[:, kt * 128:(kt + 1) * 128],
                          in_=dram_slice(qpT, kt * 128, 128, 0, 128)).then_inc(ld.h, 16)
    ld.n += 16
    nc.sync.dma_start(out=ak_s[:, :], in_=akT[:, :]).then_inc(ld.h, 16)
    ld.n += 16
    nc.sync.dma_start(out=rkT[48:49, :], in_=rel[:, :]).then_inc(ld.h, 16)
    for it in range(32):
        cs.n += 16
        nc.gpsimd.dma_start(out=av_s[:, it * E:(it + 1) * E],
                            in_=dram_slice(avals, it * 128, 128, 0, E)).then_inc(cs.h, 16)
    for kt in range(8):
        cs.n += 16
        nc.gpsimd.dma_start(out=uw1_s[:, kt * 256:(kt + 1) * 256],
                            in_=dram_slice(uw1T, kt * 128, 128, 0, 256)).then_inc(cs.h, 16)
    for kt in range(2):
        cs.n += 16
        nc.gpsimd.dma_start(out=uw2_s[:, kt:kt + 1],
                            in_=dram_slice(uw2T, kt * 128, 128, 0, 1)).then_inc(cs.h, 16)
    for e in p.eng_all():
        e.wait_ge(ld.h, ld.n); e.wait_ge(cs.h, cs.n)
    p.barrier()
    nc.vector.memset(lvc[:, :], 0.0)
    nc.vector.memset(rqT[:, :], 0.0)
    nc.vector.memset(rqpad[:, :], 0.0)
    p.chain(nc.vector)
    nc.vector.memset(rqpad[:, 48:49], 1.0)
    p.sig(nc.vector, ev)
    for nch in range(8):
        nc.tensor.matmul(ps1[0:48, :], rw_s[:, 0:48],
                         ak_s[:, nch * 512:(nch + 1) * 512], start=True, stop=True)
        p.sig(nc.tensor, mm)
        nc.vector.wait_ge(mm.h, mm.n)
        nc.vector.tensor_scalar(out=rkT[0:48, nch * 512:(nch + 1) * 512],
                                in0=ps1[0:48, :], scalar1=float(1.0 / math.sqrt(48)),
                                scalar2=None, op0=ALU.mult)
        p.sig(nc.vector, ev)
        ev.wait(nc.tensor)
    p.barrier()
    for ti in range(16):
        r = slice(ti * 128, (ti + 1) * 128)
        ev.wait(nc.sync)
        nc.sync.wait_ge(st.h, st.n)
        ld.n += 16
        nc.sync.dma_start(out=xv[:, :], in_=x_in[r, :]).then_inc(ld.h, 16)
        ld.n += 16
        nc.sync.dma_start(out=w1s[:, :], in_=pA[r, :]).then_inc(ld.h, 16)
        ld.n += 16
        nc.sync.dma_start(out=w2s[:, :], in_=pB[r, :]).then_inc(ld.h, 16)
        nc.vector.wait_ge(ld.h, ld.n)
        nc.vector.tensor_add(out=xv[:, :], in0=xv[:, :], in1=w1s[:, :])
        p.chain(nc.vector)
        nc.vector.tensor_add(out=xv[:, :], in0=xv[:, :], in1=w2s[:, :])
        p.sig(nc.vector, ev)
        ev.wait(nc.sync)
        st.n += 16
        nc.sync.dma_start(out=x7_out[r, :], in_=xv[:, :]).then_inc(st.h, 16)
        ev.wait(nc.scalar)
        nc.scalar.wait_ge(st.h, st.n - 16)
        nc.scalar.wait_ge(mm.h, mm.n)
        bcol, scol = emit_ln_stats(p, xv[:, :], stats, 0, E, 1e-5)
        c1 = stats[:, 1:2]; c3 = stats[:, 3:4]; c6 = stats[:, 6:7]
        nc.vector.tensor_mul(out=c6, in0=c3, in1=c3)
        p.chain(nc.vector)
        nc.vector.tensor_mul(out=c6, in0=c6, in1=c1)
        p.sig(nc.vector, ev)
        ev.wait(nc.scalar)
        nc.scalar.activation(out=lvc[:, ti:ti + 1], in_=c6, func=AF.Ln,
                             bias=cone[:, 0:1])
        p.sig(nc.scalar, ev)
        ev.wait(nc.vector)
        nc.vector.wait_ge(mm.h, mm.n)
        nc.vector.tensor_scalar(out=hn[:, :], in0=xv[:, :], scalar1=scol,
                                scalar2=bcol, op0=ALU.mult, op1=ALU.add)
        p.chain(nc.vector)
        nc.vector.tensor_copy(out=hnb[:, :], in_=hn[:, :])
        p.sig(nc.vector, ev)
        ev.wait(nc.tensor)
        for kt in range(8):
            nc.tensor.transpose(ptf[:, :], hn[:, kt * 128:(kt + 1) * 128],
                                idnf[:, :])
            p.sig(nc.tensor, mm)
            nc.vector.wait_ge(mm.h, mm.n)
            nc.vector.tensor_copy(out=hnT32[:, :], in_=ptf[:, :])
            p.sig(nc.vector, ev)
            ev.wait(nc.tensor)
            nc.tensor.matmul(ps1[:, 0:128], hnT32[:, :], qp_s[:, kt * 128:(kt + 1) * 128],
                             start=(kt == 0), stop=(kt == 7))
            p.sig(nc.tensor, mm)
            if kt < 7:
                nc.vector.wait_ge(mm.h, mm.n)
        nc.vector.wait_ge(mm.h, mm.n)
        nc.vector.tensor_copy(out=qrow[:, :], in_=ps1[:, 0:128])
        p.sig(nc.vector, ev)
        ev.wait(nc.tensor)
        nc.tensor.transpose(ptf[:, :], qrow[:, :], idnf[:, :])
        p.sig(nc.tensor, mm)
        nc.vector.wait_ge(mm.h, mm.n)
        nc.vector.tensor_copy(out=qT[:, :], in_=ptf[:, :])
        p.sig(nc.vector, ev)
        ev.wait(nc.tensor)
        nc.tensor.matmul(ps1[:, 0:48], qT[:, :], rw_s[:, 0:48], start=True, stop=True)
        p.sig(nc.tensor, mm)
        nc.vector.wait_ge(mm.h, mm.n)
        nc.vector.tensor_copy(out=rqpad[:, 0:48], in_=ps1[:, 0:48])
        p.sig(nc.vector, ev)
        ev.wait(nc.tensor)
        nc.tensor.transpose(ptf[:, :], rqpad[:, :], idnf[:, :])
        p.sig(nc.tensor, mm)
        nc.vector.wait_ge(mm.h, mm.n)
        nc.vector.tensor_copy(out=rqT[0:64, 0:128], in_=ptf[0:64, :])
        p.sig(nc.vector, ev)
        ev.wait(nc.tensor)
        for ncb in range(8):
            nc.tensor.matmul(ps2[:, :], rqT[0:49, 0:128],
                             rkT[0:49, ncb * 512:(ncb + 1) * 512],
                             start=True, stop=True)
            p.sig(nc.tensor, mm)
            nc.vector.wait_ge(mm.h, mm.n)
            nc.vector.tensor_copy(out=srow[:, ncb * 512:(ncb + 1) * 512],
                                  in_=ps2[:, :])
            p.sig(nc.vector, ev)
            ev.wait(nc.tensor)
        nc.vector.max(out=top8[:, 0:8], in_=srow[:, :])
        p.chain(nc.vector)
        nc.vector.tensor_scalar(out=m01[:, :], in0=srow[:, :],
                                scalar1=top8[:, 7:8], scalar2=None, op0=ALU.is_ge)
        p.chain(nc.vector)
        nc.vector.tensor_scalar(out=stats[:, 10:11], in0=top8[:, 0:1], scalar1=-1.0,
                                scalar2=None, op0=ALU.mult)
        p.sig(nc.vector, ev)
        ev.wait(nc.scalar)
        nc.scalar.activation(out=srow[:, :], in_=srow[:, :], func=AF.Exp,
                             bias=stats[:, 10:11])
        p.sig(nc.scalar, ev)
        ev.wait(nc.vector)
        nc.vector.tensor_mul(out=srow[:, :], in0=srow[:, :], in1=m01[:, :])
        p.chain(nc.vector)
        nc.vector.reduce_sum(stats[:, 11:12], srow[:, :], axis=AX.X)
        p.chain(nc.vector)
        nc.vector.reciprocal(out=stats[:, 11:12], in_=stats[:, 11:12])
        p.chain(nc.vector)
        nc.vector.tensor_scalar(out=wq8[:, :], in0=srow[:, :],
                                scalar1=stats[:, 11:12], scalar2=None, op0=ALU.mult)
        p.sig(nc.vector, ev)
        ev.wait(nc.tensor)
        for it in range(32):
            nc.tensor.transpose(ptb[:, :], wq8[:, it * 128:(it + 1) * 128], idn[:, :])
            p.sig(nc.tensor, mm)
            nc.vector.wait_ge(mm.h, mm.n)
            nc.vector.tensor_copy(out=wqT[:, it * 128:(it + 1) * 128], in_=ptb[:, :])
            p.sig(nc.vector, ev)
            ev.wait(nc.tensor)
        for oc in range(2):
            pso = ps1 if oc == 0 else ps3
            for it in range(32):
                nc.tensor.matmul(pso[:, :], wqT[:, it * 128:(it + 1) * 128],
                                 av_s[:, it * E + oc * 512: it * E + (oc + 1) * 512],
                                 start=(it == 0), stop=(it == 31))
        p.sig(nc.tensor, mm)
        nc.vector.wait_ge(mm.h, mm.n)
        nc.vector.wait_ge(st.h, st.n)
        nc.vector.tensor_copy(out=ob[:, 0:512], in_=ps1[:, :])
        p.chain(nc.vector)
        nc.vector.tensor_copy(out=ob[:, 512:1024], in_=ps3[:, :])
        p.sig(nc.vector, ev)
        ev.wait(nc.sync)
        st.n += 16
        nc.sync.dma_start(out=agg_out[r, :], in_=ob[:, :]).then_inc(st.h, 16)
        for kt in range(8):
            nc.tensor.transpose(ptb[:, :], hnb[:, kt * 128:(kt + 1) * 128], idn[:, :])
            p.sig(nc.tensor, mm)
            nc.vector.wait_ge(mm.h, mm.n)
            nc.vector.tensor_copy(out=hnbT[:, :], in_=ptb[:, :])
            p.sig(nc.vector, ev)
            ev.wait(nc.tensor)
            nc.tensor.matmul(ps2[:, 0:256], hnbT[:, :],
                             uw1_s[:, kt * 256:(kt + 1) * 256],
                             start=(kt == 0), stop=(kt == 7))
            p.sig(nc.tensor, mm)
            if kt < 7:
                nc.vector.wait_ge(mm.h, mm.n)
        nc.scalar.wait_ge(mm.h, mm.n)
        for j in range(2):
            nc.scalar.activation(out=h1b[:, j * 128:(j + 1) * 128],
                                 in_=ps2[:, j * 128:(j + 1) * 128], func=AF.Gelu,
                                 bias=ub1_s[:, j:j + 1])
        p.sig(nc.scalar, ev)
        ev.wait(nc.tensor)
        for j in range(2):
            nc.tensor.transpose(ptb[:, :], h1b[:, j * 128:(j + 1) * 128], idn[:, :])
            p.sig(nc.tensor, mm)
            nc.vector.wait_ge(mm.h, mm.n)
            nc.vector.tensor_copy(out=h1T[:, j * 128:(j + 1) * 128], in_=ptb[:, :])
            p.sig(nc.vector, ev)
            ev.wait(nc.tensor)
        for j in range(2):
            nc.tensor.matmul(ps2[:, 0:1], h1T[:, j * 128:(j + 1) * 128],
                             uw2_s[:, j:j + 1], start=(j == 0), stop=(j == 1))
        p.sig(nc.tensor, mm)
        nc.vector.wait_ge(mm.h, mm.n)
        nc.vector.wait_ge(st.h, st.n)
        nc.vector.tensor_copy(out=ob[:, 0:1], in_=ps2[:, 0:1])
        p.sig(nc.vector, ev)
        ev.wait(nc.sync)
        st.n += 16
        nc.sync.dma_start(out=lrn_out[r, :], in_=ob[:, 0:1]).then_inc(st.h, 16)
    p.barrier()
    nc.vector.reduce_sum(stats[:, 12:13], lvc[:, :], axis=AX.X)
    p.sig(nc.vector, ev)
    ev.wait(nc.tensor)
    nc.tensor.matmul(ps2[0:1, 0:1], stats[:, 12:13], cone[:, 0:1],
                     start=True, stop=True)
    p.sig(nc.tensor, mm)
    nc.vector.wait_ge(mm.h, mm.n)
    nc.vector.tensor_copy(out=stats[0:1, 13:14], in_=ps2[0:1, 0:1])
    p.sig(nc.vector, ev)
    ev.wait(nc.sync)
    st.n += 16
    nc.sync.dma_start(out=lvp_out[:, :], in_=stats[0:1, 13:14]).then_inc(st.h, 16)
    nc.sync.wait_ge(st.h, st.n)
    p.barrier()
    return p



def build_l3fb():
    p = Prog("l3fb")
    nc = p.nc
    x_in = p.inp("x", [TS, E])
    agg_in = p.inp("agg", [TS, E])
    lrn_in = p.inp("lrn", [TS, 1])
    lvp_in = p.inp("lvp", [1, 8])
    ub2 = p.inp("ub2", [128, 1])
    gw1 = p.inp("gw1", [128, 1])
    gb = p.inp("gb", [128, 1])
    x_out = p.outp("xo", [TS, E])
    xv = p.sb("xv", [128, E])
    ag = p.sb("ag", [128, E])
    lr = p.sb("lr", [128, 1])
    lvp_s = p.sb("lvp_s", [1, 8])
    ub2_s = p.sb("ub2_s", [128, 1])
    gw1_s = p.sb("gw1_s", [128, 1])
    gb_s = p.sb("gb_s", [128, 1])
    gmr = p.sb("gmr", [128, 1])
    ones_r = p.sb("ones_r", [1, 128])
    hn = p.sb("hn", [128, E])
    p._lnsq = hn
    stats = p.sb("stats", [128, 16])
    ld = p.sem("ld"); st = p.sem("st"); mm = p.sem("mm")
    ev = MSem(p, "ev")
    ps1 = p.ps("ps1", [128, 512])
    p.constcol(1e-5); p.constcol(-0.5); cone = p.constcol(1.0)
    nc.vector.memset(ones_r[:, :], 1.0)
    for (srcb, dstb) in ((lvp_in, lvp_s), (ub2, ub2_s), (gw1, gw1_s), (gb, gb_s)):
        ld.n += 16
        nc.sync.dma_start(out=dstb[:, :], in_=srcb[:, :]).then_inc(ld.h, 16)
    for e in p.eng_all():
        e.wait_ge(ld.h, ld.n)
    p.barrier()
    nc.vector.reduce_sum(lvp_s[0:1, 0:1], lvp_s[0:1, 0:8], axis=AX.X)
    p.chain(nc.vector)
    nc.vector.tensor_scalar(out=lvp_s[0:1, 0:1], in0=lvp_s[0:1, 0:1],
                            scalar1=1.0 / 16384.0, scalar2=1e-6,
                            op0=ALU.mult, op1=ALU.add)
    p.chain(nc.vector)
    nc.vector.reciprocal(out=lvp_s[0:1, 1:2], in_=lvp_s[0:1, 0:1])
    p.sig(nc.vector, ev)
    ev.wait(nc.tensor)
    nc.tensor.matmul(ps1[0:128, 0:1], ones_r[0:1, :], lvp_s[0:1, 1:2],
                     start=True, stop=True)
    p.sig(nc.tensor, mm)
    nc.vector.wait_ge(mm.h, mm.n)
    nc.vector.tensor_copy(out=gmr[:, :], in_=ps1[0:128, 0:1])
    p.sig(nc.vector, ev)
    p.barrier()
    for ti in range(16):
        r = slice(ti * 128, (ti + 1) * 128)
        ev.wait(nc.sync)
        nc.sync.wait_ge(st.h, st.n)
        ld.n += 16
        nc.sync.dma_start(out=xv[:, :], in_=x_in[r, :]).then_inc(ld.h, 16)
        ld.n += 16
        nc.sync.dma_start(out=ag[:, :], in_=agg_in[r, :]).then_inc(ld.h, 16)
        ld.n += 16
        nc.sync.dma_start(out=lr[:, :], in_=lrn_in[r, :]).then_inc(ld.h, 16)
        nc.vector.wait_ge(ld.h, ld.n)
        nc.scalar.wait_ge(ld.h, ld.n)
        bcol, scol = emit_ln_stats(p, xv[:, :], stats, 0, E, 1e-5)
        c1 = stats[:, 1:2]; c3 = stats[:, 3:4]; c6 = stats[:, 6:7]; c7 = stats[:, 7:8]
        nc.vector.tensor_mul(out=c6, in0=c3, in1=c3)
        p.chain(nc.vector)
        nc.vector.tensor_mul(out=c6, in0=c6, in1=c1)
        p.sig(nc.vector, ev)
        ev.wait(nc.scalar)
        nc.scalar.activation(out=c6, in_=c6, func=AF.Ln, bias=cone[:, 0:1])
        p.chain(nc.scalar)
        nc.scalar.activation(out=lr[:, :], in_=lr[:, :], func=AF.Sigmoid,
                             bias=ub2_s[:, 0:1])
        p.sig(nc.scalar, ev)
        ev.wait(nc.vector)
        nc.vector.tensor_mul(out=c7, in0=c6, in1=gmr[:, :])
        p.chain(nc.vector)
        nc.vector.tensor_scalar(out=c7, in0=c7, scalar1=0.5, scalar2=None,
                                op0=ALU.mult)
        p.chain(nc.vector)
        nc.vector.scalar_tensor_tensor(out=c7, in0=lr[:, :], scalar=2.5, in1=c7,
                                       op0=ALU.mult, op1=ALU.add)
        p.chain(nc.vector)
        nc.vector.tensor_scalar(out=c7, in0=c7, scalar1=0.0, scalar2=5.0,
                                op0=ALU.max, op1=ALU.min)
        p.chain(nc.vector)
        nc.vector.tensor_scalar(out=c7, in0=c7, scalar1=gw1_s[:, 0:1],
                                scalar2=gb_s[:, 0:1], op0=ALU.mult, op1=ALU.add)
        p.sig(nc.vector, ev)
        ev.wait(nc.scalar)
        nc.scalar.activation(out=c7, in_=c7, func=AF.Sigmoid)
        p.sig(nc.scalar, ev)
        ev.wait(nc.vector)
        nc.vector.tensor_scalar(out=hn[:, :], in0=xv[:, :], scalar1=scol,
                                scalar2=bcol, op0=ALU.mult, op1=ALU.add)
        p.chain(nc.vector)
        nc.vector.tensor_add(out=xv[:, :], in0=xv[:, :], in1=hn[:, :])
        p.chain(nc.vector)
        nc.vector.tensor_scalar(out=ag[:, :], in0=ag[:, :], scalar1=c7,
                                scalar2=None, op0=ALU.mult)
        p.chain(nc.vector)
        nc.vector.tensor_add(out=xv[:, :], in0=xv[:, :], in1=ag[:, :])
        p.sig(nc.vector, ev)
        ev.wait(nc.sync)
        st.n += 16
        nc.sync.dma_start(out=x_out[r, :], in_=xv[:, :]).then_inc(st.h, 16)
    nc.sync.wait_ge(st.h, st.n)
    p.barrier()
    return p
